# version 09 (final)
# speedup vs baseline: 1.2055x; 1.0101x over previous
import numpy as np
import concourse.bass as bass
import concourse.mybir as mybir

ENGS = ("sync", "tensor", "vector", "scalar", "gpsimd")

class Op:
    __slots__ = ("idx", "eng", "fn", "reads", "writes", "dma", "tag", "lp", "fence")
    def __init__(self, idx, eng, fn, reads, writes, dma, tag, lp=False, fence=False):
        self.idx = idx
        self.eng = eng
        self.fn = fn
        self.reads = reads
        self.writes = writes
        self.dma = dma
        self.tag = tag
        self.lp = lp
        self.fence = fence

class MK:
    def __init__(self, nc):
        self.nc = nc
        self.ops = []
        self.last_writer = {}
        self.readers = {}
        self._sems = []
        self.sem = {}
        self.NPOOL = 10
        names = ["p_tensor", "p_vector", "p_scalar", "p_gpc"]
        names += [f"ds{i}" for i in range(self.NPOOL)]
        names += [f"dg{i}" for i in range(self.NPOOL)]
        for name in names:
            cm = nc.semaphore(name)
            self.sem[name] = cm.__enter__()
            self._sems.append(cm)
        self.dma_count = {"sync": 0, "gpsimd": 0}
        self.op_sem = {}

    def fence(self):
        for eng in ENGS:
            i = len(self.ops)
            self.ops.append(Op(i, eng, None, (), (), False, "fence", fence=True))

    def op(self, eng, fn, reads=(), writes=(), dma=False, tag="", lp=False):
        i = len(self.ops)
        o = Op(i, eng, fn, tuple(reads), tuple(writes), dma, tag, lp)
        self.ops.append(o)
        if dma:
            assert eng in ("sync", "gpsimd"), eng
            n = self.dma_count[eng]
            self.dma_count[eng] = n + 1
            pref = "ds" if eng == "sync" else "dg"
            sn = f"{pref}{n % self.NPOOL}"
            use = n // self.NPOOL + 1
            self.op_sem[i] = (sn, 16 * use, use)
        return o

    def lower(self, verbose=False):
        nc = self.nc
        deps = [set() for _ in self.ops]
        last_writer = {}
        readers = {}
        writers = {}
        writers_prev = {}
        for o in self.ops:
            for k in o.reads:
                for w in writers.get(k, ()):
                    deps[o.idx].add(w)
                readers.setdefault(k, []).append(o.idx)
            for k in o.writes:
                rs = [r for r in readers.get(k, ()) if r != o.idx]
                if rs:
                    for r in rs:
                        deps[o.idx].add(r)
                    for w in writers.get(k, ()):
                        deps[o.idx].add(w)
                    writers_prev[k] = writers.get(k, []) + rs
                    writers[k] = [o.idx]
                    readers[k] = []
                else:
                    for w in writers_prev.get(k, ()):
                        deps[o.idx].add(w)
                    writers.setdefault(k, []).append(o.idx)
            deps[o.idx].discard(o.idx)
        semname = [None] * len(self.ops)
        semval_after = [0] * len(self.ops)
        snap = [None] * len(self.ops)
        cnt = {s: 0 for s in self.sem}
        for o in self.ops:
            if o.fence:
                snap[o.idx] = dict(cnt)
                semname[o.idx] = "p_gpc" if o.eng == "gpsimd" else ("p_" + o.eng if o.eng != "sync" else "p_tensor")
                continue
            if o.dma:
                sn, v, _u = self.op_sem[o.idx]
                semname[o.idx] = sn
                semval_after[o.idx] = v
                cnt[sn] = max(cnt[sn], v)
            else:
                sn = "p_gpc" if o.eng == "gpsimd" else "p_" + o.eng
                cnt[sn] += 1
                semname[o.idx] = sn
                semval_after[o.idx] = cnt[sn]
        self.final_counts = dict(cnt)
        ops_by_eng = {e: [o for o in self.ops if o.eng == e] for e in ENGS}
        nwaits = [0]

        def run_stream(engh, eng_name):
            waited = {}
            for o in ops_by_eng[eng_name]:
                if o.fence:
                    own = "p_gpc" if eng_name == "gpsimd" else "p_" + eng_name
                    for sn, v in sorted(snap[o.idx].items()):
                        if v and waited.get(sn, 0) < v:
                            engh.wait_ge(self.sem[sn], v)
                            waited[sn] = v
                            nwaits[0] += 1
                    continue
                need = {}
                for d in deps[o.idx]:
                    od = self.ops[d]
                    sn = semname[od.idx]
                    v = semval_after[od.idx]
                    if need.get(sn, 0) < v:
                        need[sn] = v
                if o.dma:
                    snq, vq, use = self.op_sem[o.idx]
                    if use > 1 and need.get(snq, 0) < 16 * (use - 1):
                        need[snq] = 16 * (use - 1)
                own = semname[o.idx]
                if not o.dma and need.get(own, -1) >= semval_after[o.idx]:
                    bad = [d for d in deps[o.idx]
                           if semname[self.ops[d].idx] == own and semval_after[d] >= semval_after[o.idx]]
                    raise RuntimeError(
                        f"self-deadlock: op {o.idx} tag={o.tag} eng={o.eng} "
                        f"own={semval_after[o.idx]} needs {need[own]} deps={[(d, self.ops[d].tag, self.ops[d].eng, semval_after[d]) for d in bad]}")
                for sn, v in sorted(need.items()):
                    if waited.get(sn, 0) >= v:
                        continue
                    engh.wait_ge(self.sem[sn], v)
                    waited[sn] = v
                    nwaits[0] += 1
                if o.lp:
                    with nc.allow_low_precision(reason="deliberate bf16"):
                        inst = o.fn(engh)
                else:
                    inst = o.fn(engh)
                s = semname[o.idx]
                amt = 16 if o.dma else 1
                if inst is None:
                    inst = engh.engine_nop() if eng_name != "sync" else engh.nop()
                inst.then_inc(self.sem[s], amt)

        with nc.Block() as blk:
            @blk.sync
            def _(sync):
                run_stream(sync, "sync")
                for sn, v in sorted(self.final_counts.items()):
                    if v:
                        sync.wait_ge(self.sem[sn], v)
            @blk.tensor
            def _(t):
                run_stream(t, "tensor")
            @blk.vector
            def _(v):
                run_stream(v, "vector")
            @blk.scalar
            def _(s):
                run_stream(s, "scalar")
            @blk.gpsimd
            def _(g):
                run_stream(g, "gpsimd")
        if verbose:
            print(f"MK: {len(self.ops)} ops, {nwaits[0]} waits; sems={self.final_counts}")

    def free_sems(self):
        for cm in self._sems:
            cm.__exit__(None, None, None)


from concourse.bass_core_rust import IndirectOffsetOnAxis

F32 = mybir.dt.float32
BF16 = mybir.dt.bfloat16
I32 = mybir.dt.int32
AX = mybir.AxisListType
ALU = mybir.AluOpType
ACT = mybir.ActivationFunctionType

T, D, NCH, NB = 4096, 1024, 8, 8
SPAN, HALO = 512, 2
CL = 512
TRASH = 16 * 512
NSPAN = T // SPAN
SW = SPAN + 2 * HALO
NEG = -30000.0

def host_consts():
    c = {}
    c["ident"] = np.eye(128, dtype=np.float32)
    c["onesmat"] = np.ones((128, 128), dtype=np.float32)
    m = np.full((128, 132), NEG, np.float32)
    for i in range(128):
        m[i, i:i + 5] = 0.0
    first = m.copy(); first[0, 0:2] = NEG; first[1, 1:2] = NEG
    last = m.copy(); last[127, 130:132] = NEG; last[126, 130:131] = NEG
    c["band"] = np.concatenate([first, m, last], axis=1)
    return c

class KB:
    def __init__(self, dbg=()):
        self.nc = bass.Bass("TRN2", target_bir_lowering=False, debug=False)
        self.mk = MK(self.nc)
        self.dbg_names = set(dbg)
        self.dbg = {}
        self.guards = []
        self.inputs = {}

    def inp(self, name, shape, dtype=F32):
        ap = self.nc.dram_tensor(name, list(shape), dtype, kind="ExternalInput").ap()
        self.inputs[name] = ap
        return ap

    def dram(self, name, shape, dtype=F32):
        return self.nc.dram_tensor(name, list(shape), dtype, kind="Internal").ap()

    def out(self, name, shape, dtype=F32):
        return self.nc.dram_tensor(name, list(shape), dtype, kind="ExternalOutput").ap()

    def sb(self, name, shape, dtype, scope=None):
        cm = self.nc.sbuf_tensor(name, shape, dtype)
        h = cm.__enter__()
        (self.guards if scope is None else scope).append(cm)
        return h

    def close_scope(self, scope):
        for cm in reversed(scope):
            cm.__exit__(None, None, None)
        scope.clear()
        self.mk.fence()

    def ps(self, name, shape, dtype=F32):
        cm = self.nc.psum_tensor(name, shape, dtype)
        h = cm.__enter__()
        self.guards.append(cm)
        return h

    def declare(self):
        nc = self.nc
        self.xown = self.inp("xown_fm", [D, T])
        self.Wq = self.inp("Wq", [D, D])
        self.Wk = self.inp("Wk", [D, D])
        self.Wv = self.inp("Wv", [D, D])
        self.bq = self.inp("bq_pc", [128, NCH])
        self.bk = self.inp("bk_pc", [128, NCH])
        self.bv = self.inp("bv_pc", [128, NCH])
        self.xall = self.inp("xall_fm", [NB, D, T])
        self.WqT = self.inp("WqT", [D, D])
        self.WkT = self.inp("WkT", [D, D])
        self.WvT = self.inp("WvT", [D, D])
        self.Wg = self.inp("Wg", [D, 16])
        self.prefmask = self.inp("prefmask", [16, NB])
        self.ebase = self.inp("ebase", [16, 1])
        self.w1 = self.inp("w1", [16, D, 4096])
        self.w2 = self.inp("w2", [16, 2, 16, 128, 1024])
        self.wqkv = self.inp("w_qkv", [D, 3072])
        self.wfc = self.inp("w_fc", [D, D])
        self.bfc = self.inp("bfc_pc", [128, NCH])
        self.wconvr = self.inp("wconv_row", [1, 208])
        self.scur = self.inp("c_scur", [13, 128, 128])
        self.rwT = self.inp("rwT", [D, 8])
        self.rbias = self.inp("rbias", [1, 8])
        self.wgT = self.inp("wgT", [8, 4, 128, D])
        self.wuT = self.inp("wuT", [8, 4, 128, D])
        self.wdT = self.inp("wdT", [8, 512, D])
        self.sgT = self.inp("sgT", [8, 128, D])
        self.suT = self.inp("suT", [8, 128, D])
        self.sdT = self.inp("sdT", [D, D])
        self.sprev = self.inp("c_sprev", [13, 128, 128])
        self.snext = self.inp("c_snext", [13, 128, 128])
        self.cband = self.inp("c_band", [128, 396])
        self.cident = self.inp("c_ident", [128, 128])
        self.cones = self.inp("c_onesmat", [128, 128])
        self.OUT = self.out("out", [T, D])
        self.resbuf = self.dram("resbuf", [128, NCH * T])

        self.ident = self.sb("identb", [128, 128], BF16)
        self.onesb = self.sb("onesbb", [128, 128], BF16)
        self.band = self.sb("band", [128, 396], F32)
        self.bqv = self.sb("bqv", [128, NCH], F32)
        self.bkv = self.sb("bkv", [128, NCH], F32)
        self.bvv = self.sb("bvv", [128, NCH], F32)
        self.sqt = self.sb("sqt", [128, NCH * SW], BF16)
        self.mu_t = self.sb("mu_t", [128, SW], BF16)
        self.inv_t = self.sb("inv_t", [128, SW], BF16)
        self.var_t = self.sb("var_t", [128, SW], F32)
        self.ptilep = [self.sb(f"ptile{i}", [128, 132], F32) for i in range(2)]
        self.ptbp = [self.sb(f"ptb{i}", [128, 132], BF16) for i in range(2)]
        self.ptTp = [self.sb(f"ptT{i}", [128, 128], BF16) for i in range(2)]
        self.ptT2p = [self.sb(f"ptT2{i}", [4, 128], BF16) for i in range(2)]
        self.ptile, self.ptb, self.ptT, self.ptT2 = (self.ptilep[0], self.ptbp[0],
                                                     self.ptTp[0], self.ptT2p[0])
        self.rowmxp = [self.sb(f"rowmx{i}", [128, 8], F32) for i in range(2)]
        self.rowmx = self.rowmxp[0]
        self.epsv = self.sb("epsv", [128, 1], F32)

        self.wg_sb = self.sb("wg_sb", [128, NCH * 16], BF16)
        self.wvg = self.sb("wvg", [128, NCH * 16], BF16)
        self.u2b = self.sb("u2b", [128, NCH], BF16)
        self.wvmb = self.sb("wvmb", [128, NCH], BF16)
        self.gcsb = self.sb("gcsb", [128, 16], BF16)
        self.bvgrow = self.sb("bvgrow", [1, 16], BF16)
        self.negbvm = self.sb("negbvm", [128, 1], F32)
        self.bvb = self.sb("bvb", [128, NCH], BF16)
        self.bqvb = self.sb("bqvb", [128, NCH], BF16)
        self.gcsrow16b = self.sb("gcsrow16b", [16, 1], BF16)
        self.bvgcol16b = self.sb("bvgcol16b", [16, 1], BF16)
        self.growb = self.sb("growb", [1, 16], BF16)
        self.bvsumb = self.sb("bvsumb", [128, 1], BF16)
        self.bvmrow = self.sb("bvmrow", [1, 1], BF16)
        self.oodb = self.sb("oodb", [128, 1], BF16)
        self.onesf = self.sb("onesf", [16, 1], F32)
        self.offs = self.sb("offs", [16, 1], F32)
        self.ohtok = self.sb("ohtok", [128, 32 * 16], BF16)
        self.cnt_sb = self.sb("cnt_sb", [16, NB], F32)
        self.gatecol = self.sb("gatecol", [128, 32], F32)
        self.bfcv = self.sb("bfcv", [128, NCH], F32)
        self.PS_A = self.ps("PS_A", [128, 1024])
        self.PS_B = self.ps("PS_B", [128, 1024])
        self.PS_C = self.ps("PS_C", [128, 512])
        self.PS_D = self.ps("PS_D", [128, 512])
        self.PS_E = self.ps("PS_E", [128, 512])
        self.PS_T = self.ps("PS_T", [128, 1024], BF16)
        self.scopeXN = []
        self.xnwp = [self.sb(f"xnw{i}", [128, NCH * SW], BF16, scope=self.scopeXN) for i in range(2)]
        self.xnw = self.xnwp[0]
        self.scopeA = []
        self.wq_sb = self.sb("wq_sb", [128, 64 * 128], BF16, scope=self.scopeA)
        self.wk_sb = self.sb("wk_sb", [128, 64 * 128], BF16, scope=self.scopeA)
        self.wv_sb = self.sb("wv_sb", [128, NCH * D], BF16, scope=self.scopeA)
        self.xw = [self.sb(f"xw{i}", [128, NCH * SW], F32, scope=self.scopeA) for i in range(2)]
        self.q_spp = [self.sb(f"q_sp{i}", [128, NCH * SW], BF16, scope=self.scopeA) for i in range(2)]
        self.k_spp = [self.sb(f"k_sp{i}", [128, NCH * SW], BF16, scope=self.scopeA) for i in range(2)]
        self.vtp = [self.sb(f"vt{i}", [128, 5 * D], BF16, scope=self.scopeA) for i in range(2)]
        self.resw = self.sb("resw", [128, NCH * SPAN], F32, scope=self.scopeA)


    def s0(self):
        mk, nc = self.mk, self.nc
        mk.op("vector", lambda v: v.memset(self.epsv[:], 1e-5), writes=["epsv"])
        mk.op("gpsimd", lambda g: g.dma_start(out=self.ident[:], in_=self.cident),
              writes=["ident"], dma=True, tag="ld ident")
        mk.op("gpsimd", lambda g: g.dma_start(out=self.onesb[:], in_=self.cones),
              writes=["ones"], dma=True, tag="ld ones")
        mk.op("sync", lambda s: s.dma_start(out=self.band[:], in_=self.cband),
              writes=["band"], dma=True)
        mk.op("sync", lambda s: s.dma_start(out=self.bqv[:], in_=self.bq),
              writes=["bqv"], dma=True)
        mk.op("sync", lambda s: s.dma_start(out=self.bkv[:], in_=self.bk),
              writes=["bkv"], dma=True)
        mk.op("sync", lambda s: s.dma_start(out=self.bvv[:], in_=self.bv),
              writes=["bvv"], dma=True)
        mk.op("sync", lambda s: s.dma_start(out=self.bfcv[:], in_=self.bfc),
              writes=["bfcv"], dma=True)
        for ic in range(NCH):
            mk.op("gpsimd", lambda g, ic=ic: g.dma_start(
                out=self.wv_sb[:, ic * D:(ic + 1) * D],
                in_=self.Wv[128 * ic:128 * ic + 128, :]),
                writes=[("wv", ic)], dma=True)
        for W, buf, key in ((self.Wq, self.wq_sb, "wq"), (self.Wk, self.wk_sb, "wk")):
            for ic in range(NCH):
                for oc in range(NCH):
                    col = (ic * NCH + oc) * 128
                    mk.op("gpsimd",
                          lambda g, W=W, buf=buf, ic=ic, oc=oc, col=col:
                          g.dma_start(out=buf[:, col:col + 128],
                                      in_=W[128 * ic:128 * ic + 128, 128 * oc:128 * oc + 128]),
                          writes=[(key, ic, oc)], dma=True)

    def s0_routing(self):
        mk = self.mk
        mk.op("vector", lambda v: v.tensor_copy(self.bqvb[:], self.bqv[:]),
              reads=["bqv"], writes=["bqvb"], lp=True)
        mk.op("vector", lambda v: v.tensor_copy(self.bvb[:], self.bvv[:]),
              reads=["bvv"], writes=["bvb"], lp=True)
        for kc in range(NCH):
            mk.op("gpsimd", lambda g, kc=kc: g.dma_start(
                out=self.wg_sb[:, kc * 16:(kc + 1) * 16], in_=self.Wg[128 * kc:128 * kc + 128, :]),
                writes=[("wg", kc)], dma=True)
        wtA = self.sb("wtA", [128, 2 * 128], BF16, scope=self.scopeB)
        wtB = self.sb("wtB", [128, 2 * D], BF16, scope=self.scopeB)

        for gc in range(NCH):
            for kc in range(NCH):
                mk.op("gpsimd", lambda g, gc=gc, kc=kc: g.dma_start(
                    out=wtA[:, (kc % 2) * 128:(kc % 2) * 128 + 128],
                    in_=self.WkT[128 * kc:128 * kc + 128, 128 * gc:128 * gc + 128]),
                    writes=[("wtA", kc % 2)], reads=[("wtA_c", kc % 2)], dma=True)
                mk.op("gpsimd", lambda g, kc=kc: g.dma_start(
                    out=wtB[:, (kc % 2) * D:(kc % 2) * D + D],
                    in_=self.WqT[128 * kc:128 * kc + 128, :]),
                    writes=[("wtB", kc % 2)], reads=[("wtB_c", kc % 2)], dma=True)
                def mmbt(t, gc=gc, kc=kc):
                    last = (kc == NCH - 1)
                    t.matmul(self.PS_A[:, 0:512], wtA[:, (kc % 2) * 128:(kc % 2) * 128 + 128],
                             wtB[:, (kc % 2) * D:(kc % 2) * D + 512],
                             start=(kc == 0), stop=last)
                    return t.matmul(self.PS_A[:, 512:1024], wtA[:, (kc % 2) * 128:(kc % 2) * 128 + 128],
                                    wtB[:, (kc % 2) * D + 512:(kc % 2) * D + D],
                                    start=(kc == 0), stop=last)
                mk.op("tensor", mmbt, reads=[("wtA", kc % 2), ("wtB", kc % 2)],
                      writes=["psA", ("wtA_c", kc % 2), ("wtB_c", kc % 2)], tag=f"bt g{gc}k{kc}")
            mk.op("vector", lambda v, gc=gc: v.tensor_copy(self.BTs[:, gc * D:(gc + 1) * D],
                                                           self.PS_A[:, 0:1024]),
                  reads=["psA"], writes=[("bts", gc)], tag="evacBT", lp=True)
        for fc in range(NCH):
            for kc in range(NCH):
                mk.op("gpsimd", lambda g, fc=fc, kc=kc: g.dma_start(
                    out=wtA[:, (kc % 2) * 128:(kc % 2) * 128 + 128],
                    in_=self.WvT[128 * kc:128 * kc + 128, 128 * fc:128 * fc + 128]),
                    writes=[("wtA", kc % 2)], reads=[("wtA_c", kc % 2)], dma=True)
                def mmwvg(t, fc=fc, kc=kc):
                    last = (kc == NCH - 1)
                    t.matmul(self.PS_C[:, 0:16], wtA[:, (kc % 2) * 128:(kc % 2) * 128 + 128],
                             self.wg_sb[:, kc * 16:(kc + 1) * 16],
                             start=(kc == 0), stop=last)
                    return t.matmul(self.PS_D[:, 0:1], wtA[:, (kc % 2) * 128:(kc % 2) * 128 + 128],
                                    self.onesb[:, 0:1], start=(kc == 0), stop=last)
                mk.op("tensor", mmwvg, reads=[("wtA", kc % 2), ("wg", kc), "ones"],
                      writes=["psC", "psD", ("wtA_c", kc % 2)])
            mk.op("vector", lambda v, fc=fc: v.tensor_copy(self.wvg[:, fc * 16:(fc + 1) * 16],
                                                           self.PS_C[:, 0:16]),
                  reads=["psC"], writes=["wvg"], lp=True)
            mk.op("scalar", lambda sc, fc=fc: sc.activation(
                self.wvmb[:, fc:fc + 1], self.PS_D[:, 0:1], ACT.Identity, scale=1.0 / D),
                reads=["psD"], writes=["wvmb"], lp=True)
        for fc in range(NCH):
            for kc in range(NCH):
                mk.op("gpsimd", lambda g, fc=fc, kc=kc: g.dma_start(
                    out=wtA[:, (kc % 2) * 128:(kc % 2) * 128 + 128],
                    in_=self.WkT[128 * kc:128 * kc + 128, 128 * fc:128 * fc + 128]),
                    writes=[("wtA", kc % 2)], reads=[("wtA_c", kc % 2)], dma=True)
                mk.op("tensor", lambda t, fc=fc, kc=kc: t.matmul(
                    self.PS_D[:, 0:1], wtA[:, (kc % 2) * 128:(kc % 2) * 128 + 128],
                    self.bqvb[:, kc:kc + 1],
                    start=(kc == 0), stop=(kc == NCH - 1)),
                    reads=[("wtA", kc % 2), "bqvb"], writes=["psD", ("wtA_c", kc % 2)])
            mk.op("vector", lambda v, fc=fc: v.tensor_copy(self.u2b[:, fc:fc + 1], self.PS_D[:, 0:1]),
                  reads=["psD"], writes=["u2b"], lp=True)
        def mmgcs(t):
            for kc in range(NCH):
                li = t.matmul(self.PS_C[0:16, 32:33], self.wg_sb[:, kc * 16:(kc + 1) * 16],
                              self.onesb[:, 0:1], start=(kc == 0), stop=(kc == NCH - 1))
            return li
        mk.op("tensor", mmgcs, reads=[("wg", kc) for kc in range(NCH)] + ["ones"],
              writes=["psC"], tag="gcs")
        def mmbvg(t):
            for kc in range(NCH):
                li = t.matmul(self.PS_C[0:16, 33:34], self.wg_sb[:, kc * 16:(kc + 1) * 16],
                              self.bvb[:, kc:kc + 1], start=(kc == 0), stop=(kc == NCH - 1))
            return li
        mk.op("tensor", mmbvg, reads=[("wg", kc) for kc in range(NCH)] + ["bvb"],
              writes=["psC"], tag="bvg")
        mk.op("vector", lambda v: v.tensor_copy(self.gcsrow16b[0:16, 0:1], self.PS_C[0:16, 32:33]),
              reads=["psC"], writes=["gcsrow16b"], lp=True)
        mk.op("vector", lambda v: v.tensor_copy(self.bvgcol16b[0:16, 0:1], self.PS_C[0:16, 33:34]),
              reads=["psC"], writes=["bvgcol16b"], lp=True)
        mk.op("tensor", lambda t: t.transpose(self.PS_T[0:1, 0:16], self.gcsrow16b[0:16, 0:1],
                                              self.ident[0:16, 0:16]),
              reads=["gcsrow16b", "ident"], writes=["psT0"], lp=True)
        mk.op("vector", lambda v: v.tensor_copy(self.growb[0:1, 0:16], self.PS_T[0:1, 0:16]),
              reads=["psT0"], writes=["growb"], lp=True)
        mk.op("tensor", lambda t: t.transpose(self.PS_T[0:1, 16:32], self.bvgcol16b[0:16, 0:1],
                                              self.ident[0:16, 0:16]),
              reads=["bvgcol16b", "ident"], writes=["psT0"], lp=True)
        mk.op("vector", lambda v: v.tensor_copy(self.bvgrow[0:1, 0:16], self.PS_T[0:1, 16:32]),
              reads=["psT0"], writes=["bvgrow"], lp=True)
        mk.op("tensor", lambda t: t.matmul(self.PS_C[:, 34:50], self.onesb[0:1, :],
                                           self.growb[0:1, 0:16], start=True, stop=True),
              reads=["growb", "ones"], writes=["psC"])
        mk.op("vector", lambda v: v.tensor_copy(self.gcsb[:], self.PS_C[:, 34:50]),
              reads=["psC"], writes=["gcsb"], lp=True)
        mk.op("vector", lambda v: v.reduce_sum(self.rowmx[:, 3:4], self.bvv[:], axis=AX.X),
              reads=["bvv"], writes=["rowmx4"])
        mk.op("vector", lambda v: v.tensor_copy(self.bvsumb[:, 0:1], self.rowmx[:, 3:4]),
              reads=["rowmx4"], writes=["bvsumb"], lp=True)
        mk.op("tensor", lambda t: t.matmul(self.PS_C[0:1, 50:51], self.bvsumb[:, 0:1],
                                           self.onesb[:, 0:1], start=True, stop=True),
              reads=["bvsumb", "ones"], writes=["psC"])
        mk.op("scalar", lambda sc: sc.activation(self.bvmrow[0:1, 0:1], self.PS_C[0:1, 50:51],
                                                 ACT.Identity, scale=1.0 / D),
              reads=["psC"], writes=["bvmrow"], lp=True)
        mk.op("tensor", lambda t: t.matmul(self.PS_C[:, 51:52], self.onesb[0:1, :],
                                           self.bvmrow[0:1, 0:1], start=True, stop=True),
              reads=["bvmrow", "ones"], writes=["psC"])
        mk.op("scalar", lambda sc: sc.activation(self.negbvm[:], self.PS_C[:, 51:52],
                                                 ACT.Identity, scale=-1.0),
              reads=["psC"], writes=["negbvm"])

    def s1(self):
        self._s1_spanA(0)
        for s in range(NSPAN):
            if s + 1 < NSPAN:
                self._s1_spanA(s + 1)
            self._s1_spanB(s)
        if "y" in self.dbg_names:
            d = self.out("dbg_y", [128, NCH * T])
            self.dbg["y"] = d
            mk = self.mk
            for c in range(NCH):
                mk.op("sync", lambda s, c=c: s.dma_start(out=d[:, c * T:(c + 1) * T],
                                                         in_=self.resbuf[:, c * T:(c + 1) * T]),
                      reads=[("resb", sp) for sp in range(NSPAN)], dma=True)

    def _s1_spanA(self, s):
        mk, nc = self.mk, self.nc
        P = s % 2
        xw = self.xw[P]
        xwk = ("xw", P)
        xnw_ = self.xnwp[P]
        t0 = s * SPAN
        lo = max(0, t0 - HALO)
        hi = min(T, t0 + SPAN + HALO)
        c0 = lo - (t0 - HALO)
        c1 = hi - (t0 - HALO)
        for c in range(NCH):
            if c0 > 0:
                mk.op("vector", lambda v, c=c: v.memset(xw[:, c * SW:c * SW + c0], 0.0),
                      writes=[xwk], tag=f"memset xw lo s{s}")
            if c1 < SW:
                mk.op("vector", lambda v, c=c: v.memset(xw[:, c * SW + c1:(c + 1) * SW], 0.0),
                      writes=[xwk])
            mk.op("sync", lambda ss, c=c: ss.dma_start(
                out=xw[:, c * SW + c0:c * SW + c1],
                in_=self.xown[128 * c:128 * c + 128, lo:hi]),
                writes=[xwk], dma=True, tag=f"ld xw s{s}c{c}")
        def vcast(v):
            for c in range(NCH):
                li = v.tensor_copy(xnw_[:, c * SW:(c + 1) * SW], xw[:, c * SW:(c + 1) * SW])
            return li
        mk.op("vector", vcast, reads=[xwk], writes=[("xnw", P)], tag="cast")
        def vsq(v):
            for c in range(NCH):
                li = v.tensor_mul(self.sqt[:, c * SW:(c + 1) * SW], xnw_[:, c * SW:(c + 1) * SW],
                                  xnw_[:, c * SW:(c + 1) * SW])
            return li
        mk.op("vector", vsq, reads=[("xnw", P)], writes=["sqt"], lp=True)
        def mm_stats(t):
            for c in range(NCH):
                last = (c == NCH - 1)
                t.matmul(self.PS_A[:, 0:512], self.onesb[:], xnw_[:, c * SW:c * SW + 512],
                         start=(c == 0), stop=last)
                t.matmul(self.PS_A[:, 512:516], self.onesb[:], xnw_[:, c * SW + 512:(c + 1) * SW],
                         start=(c == 0), stop=last)
                t.matmul(self.PS_B[:, 0:512], self.onesb[:], self.sqt[:, c * SW:c * SW + 512],
                         start=(c == 0), stop=last)
                li = t.matmul(self.PS_B[:, 512:516], self.onesb[:], self.sqt[:, c * SW + 512:(c + 1) * SW],
                              start=(c == 0), stop=last)
            return li
        mk.op("tensor", mm_stats, reads=[("xnw", P), "sqt", "ones"], writes=["psA", "psB"],
              tag=f"stats s{s}")
        mk.op("scalar", lambda sc: sc.mul(self.mu_t[:], self.PS_A[:, 0:SW], 1.0 / D),
              reads=["psA"], writes=["mu"])
        mk.op("vector", lambda v: v.tensor_mul(self.var_t[:], self.mu_t[:], self.mu_t[:]),
              reads=["mu"], writes=["var"])
        mk.op("vector", lambda v: v.scalar_tensor_tensor(self.var_t[:], self.PS_B[:, 0:SW],
                                                         1.0 / D, self.var_t[:],
                                                         ALU.mult, ALU.subtract),
              reads=["psB", "var"], writes=["var"])
        mk.op("scalar", lambda sc: sc.activation(self.var_t[:], self.var_t[:], ACT.Sqrt, bias=self.epsv[:, 0:1]),
              reads=["var", "epsv"], writes=["var"])
        mk.op("vector", lambda v: v.reciprocal(self.inv_t[:], self.var_t[:]),
              reads=["var"], writes=["inv"], lp=True)
        def vsub(v):
            for c in range(NCH):
                li = v.tensor_sub(xnw_[:, c * SW:(c + 1) * SW],
                                  xnw_[:, c * SW:(c + 1) * SW], self.mu_t[:])
            return li
        mk.op("vector", vsub, reads=["mu", ("xnw", P)], writes=[("xnw", P)])
        def vmul(v):
            for c in range(NCH):
                li = v.tensor_mul(xnw_[:, c * SW:(c + 1) * SW],
                                  xnw_[:, c * SW:(c + 1) * SW], self.inv_t[:])
            return li
        mk.op("vector", vmul, reads=["inv", ("xnw", P)], writes=[("xnw", P)])

    def _s1_spanB(self, s):
        mk, nc = self.mk, self.nc
        P = s % 2
        xw = self.xw[P]
        xwk = ("xw", P)
        xnw_ = self.xnwp[P]
        q_sp_ = self.q_spp[P]
        k_sp_ = self.k_spp[P]
        vt_ = self.vtp[P]
        t0 = s * SPAN
        for dstbuf, wsb, wkey, bias, bkey, okey in (
                (q_sp_, self.wq_sb, "wq", self.bqv, "bqv", "q"),
                (k_sp_, self.wk_sb, "wk", self.bkv, "bkv", "k")):
            for oc in range(NCH):
                PS = self.PS_A if oc % 2 == 0 else self.PS_B
                pk = "psA" if oc % 2 == 0 else "psB"
                def mmproj(t, wsb=wsb, oc=oc, PS=PS):
                    for ic in range(NCH):
                        col = (ic * NCH + oc) * 128
                        last = (ic == NCH - 1)
                        t.matmul(PS[:, 0:512], wsb[:, col:col + 128],
                                 xnw_[:, ic * SW:ic * SW + 512],
                                 start=(ic == 0), stop=last)
                        li = t.matmul(PS[:, 512:516], wsb[:, col:col + 128],
                                      xnw_[:, ic * SW + 512:(ic + 1) * SW],
                                      start=(ic == 0), stop=last)
                    return li
                mk.op("tensor", mmproj,
                      reads=[("xnw", P)] + [(wkey, ic, oc) for ic in range(NCH)],
                      writes=[pk], tag=f"proj{okey} s{s}o{oc}")
                mk.op("scalar", lambda sc, dstbuf=dstbuf, bias=bias, oc=oc, PS=PS:
                      sc.activation(dstbuf[:, oc * SW:(oc + 1) * SW], PS[:, 0:SW],
                                    ACT.Identity, bias=bias[:, oc:oc + 1]),
                      reads=[pk, bkey], writes=[(okey, P)], tag=f"evac{okey}")
        ntile_v = (SW + 127) // 128
        for r in range(ntile_v):
            cols = min(128, SW - r * 128)
            PS = self.PS_A if r % 2 == 0 else self.PS_B
            pk = "psA" if r % 2 == 0 else "psB"
            def mmv(t, r=r, cols=cols, PS=PS):
                for ic in range(NCH):
                    last = (ic == NCH - 1)
                    t.matmul(PS[0:cols, 0:512],
                             xnw_[:, ic * SW + r * 128: ic * SW + r * 128 + cols],
                             self.wv_sb[:, ic * D: ic * D + 512],
                             start=(ic == 0), stop=last)
                    li = t.matmul(PS[0:cols, 512:1024],
                                  xnw_[:, ic * SW + r * 128: ic * SW + r * 128 + cols],
                                  self.wv_sb[:, ic * D + 512:(ic + 1) * D],
                                  start=(ic == 0), stop=last)
                return li
            mk.op("tensor", mmv, reads=[("xnw", P)] + [("wv", ic) for ic in range(NCH)],
                  writes=[pk], tag=f"vproj s{s}r{r}")
            mk.op("vector", lambda v, r=r, cols=cols, PS=PS: v.tensor_copy(
                vt_[0:cols, r * D:(r + 1) * D], PS[0:cols, 0:1024]),
                reads=[pk], writes=[("vt", P, r)], tag="evacV")
        for tt in range(SPAN // 128):
            gt = s * 4 + tt
            TP = gt % 2
            ptile_, ptb_, ptT_, ptT2_, rowmx_ = (self.ptilep[TP], self.ptbp[TP],
                                                 self.ptTp[TP], self.ptT2p[TP], self.rowmxp[TP])
            bm = 0 if gt == 0 else (2 if gt == 31 else 1)
            w0 = tt * 128
            def mmsc(t, tt=tt, w0=w0):
                for c in range(NCH):
                    li = t.matmul(self.PS_C[:, 0:132],
                                  q_sp_[:, c * SW + HALO + tt * 128: c * SW + HALO + tt * 128 + 128],
                                  k_sp_[:, c * SW + w0: c * SW + w0 + 132],
                                  start=(c == 0), stop=(c == NCH - 1))
                return li
            mk.op("tensor", mmsc, reads=[("q", P), ("k", P)], writes=["psC"], tag=f"score s{s}t{tt}")
            mk.op("vector", lambda v, bm=bm: v.scalar_tensor_tensor(
                ptile_[:], self.PS_C[:, 0:132], 1.0 / 32.0, self.band[:, bm * 132:(bm + 1) * 132],
                ALU.mult, ALU.add),
                reads=["psC", "band"], writes=[("ptile", TP)], tag="mask")
            mk.op("vector", lambda v: v.reduce_max(rowmx_[:, 0:1], ptile_[:], axis=AX.X,
                                                   negate=True),
                  reads=[("ptile", TP)], writes=[("rowmx", TP)])
            mk.op("scalar", lambda sc: sc.activation(ptile_[:], ptile_[:], ACT.Exp,
                                                     bias=rowmx_[:, 0:1],
                                                     accum_out=rowmx_[:, 1:2]),
                  reads=[("ptile", TP), ("rowmx", TP)], writes=[("ptile", TP), ("rowmx2", TP)])
            mk.op("vector", lambda v: v.reciprocal(rowmx_[:, 2:3], rowmx_[:, 1:2]),
                  reads=[("rowmx2", TP)], writes=[("rowmx3", TP)])
            mk.op("vector", lambda v: v.tensor_scalar_mul(ptb_[:], ptile_[:], rowmx_[:, 2:3]),
                  reads=[("ptile", TP), ("rowmx3", TP)], writes=[("ptb", TP)], tag="pnorm")
            T0q = 0 if TP == 0 else 512
            tkq = f"psT{TP}"
            def mmTs1(t, T0q=T0q):
                t.transpose(self.PS_T[:, T0q:T0q + 128], ptb_[:, 0:128], self.ident[:])
                return t.transpose(self.PS_T[0:4, T0q + 128:T0q + 256], ptb_[:, 128:132],
                                   self.ident[:])
            mk.op("tensor", mmTs1, reads=[("ptb", TP), "ident"], writes=[tkq], tag="Tp", lp=True)
            def vTs1(v, T0q=T0q):
                v.tensor_copy(ptT_[:], self.PS_T[:, T0q:T0q + 128])
                return v.tensor_copy(ptT2_[:], self.PS_T[0:4, T0q + 128:T0q + 256])
            mk.op("vector", vTs1, reads=[tkq], writes=[("ptT", TP), ("ptT2", TP)])
            for oc in range(NCH):
                PSV = self.PS_E if oc % 2 == 0 else self.PS_D
                pvk = "psE" if oc % 2 == 0 else "psD"
                def mmpv(t, tt=tt, oc=oc, PSV=PSV):
                    r0, r1 = tt, tt + 1
                    t.matmul(PSV[:, 0:128], vt_[:, r0 * D + oc * 128: r0 * D + oc * 128 + 128],
                             ptT_[:], start=True, stop=False)
                    return t.matmul(PSV[:, 0:128],
                                    vt_[0:4, r1 * D + oc * 128: r1 * D + oc * 128 + 128],
                                    ptT2_[:], start=False, stop=True)
                mk.op("tensor", mmpv, reads=[("ptT", TP), ("ptT2", TP), ("vt", P, tt), ("vt", P, tt + 1)],
                      writes=[pvk], tag=f"pv s{s}t{tt}o{oc}")
                mk.op("vector", lambda v, tt=tt, oc=oc, PSV=PSV: v.scalar_tensor_tensor(
                    self.resw[:, oc * SPAN + tt * 128: oc * SPAN + tt * 128 + 128],
                    PSV[:, 0:128], self.bvv[:, oc:oc + 1],
                    xw[:, oc * SW + HALO + tt * 128: oc * SW + HALO + tt * 128 + 128],
                    ALU.add, ALU.add),
                    reads=[pvk, "bvv", xwk], writes=["resw"], tag="resadd")
        for c in range(NCH):
            mk.op("sync", lambda ss, c=c: ss.dma_start(
                out=self.resbuf[:, c * T + t0: c * T + t0 + SPAN],
                in_=self.resw[:, c * SPAN:(c + 1) * SPAN]),
                reads=["resw"], writes=[("resb", s)], dma=True, tag=f"st res s{s}")

    def s2(self, own_b, batches=None):
        mk = self.mk
        self.scopeOwn = []
        self.ohe = self.sb("ohe", [16, T], F32, scope=self.scopeOwn)
        self.locx = self.sb("locx", [16, T], F32, scope=self.scopeOwn)
        self.scopeB = []
        self.BTs = self.sb("BTs", [128, NCH * D], BF16, scope=self.scopeB)
        self.xwbp = [self.sb(f"xwb{i}", [128, NCH * SW], BF16, scope=self.scopeB) for i in range(2)]
        self.hbufp = [self.sb(f"hbuf{i}", [128, NCH * SW], BF16, scope=self.scopeB) for i in range(2)]
        self.xvbuf = self.sb("xvbuf", [128, 5 * 16], BF16, scope=self.scopeB)
        self.xwvmcb = self.sb("xwvmcb", [128, 5], BF16, scope=self.scopeB)
        self.d2b = self.sb("d2b", [1, SW], BF16, scope=self.scopeB)
        self.ztilep = [self.sb(f"ztile{i}", [128, 16], F32, scope=self.scopeB) for i in range(2)]
        self.m1colp = [self.sb(f"m1col{i}", [128, 1], F32, scope=self.scopeB) for i in range(2)]
        self.negmup = [self.sb(f"negmu{i}", [128, 1], F32, scope=self.scopeB) for i in range(2)]
        self.ohtmpp = [self.sb(f"ohtmp{i}", [128, 16], BF16, scope=self.scopeB) for i in range(2)]
        mk.op("vector", lambda v: v.memset(self.oodb[:], 1.0 / D), writes=["oodb"])
        mk.op("vector", lambda v: v.memset(self.onesf[:], 1.0), writes=["onesf"])
        mk.op("vector", lambda v: v.memset(self.cnt_sb[:], 0.0), writes=["cnt"])
        self.s0_routing()
        seq = [(b, sp) for b in (range(NB) if batches is None else batches)
               for sp in range(NSPAN)]
        self._s2_spanA(*seq[0], idx=0)
        for i, (b, sp) in enumerate(seq):
            if i + 1 < len(seq):
                self._s2_spanA(*seq[i + 1], idx=i + 1)
            self._s2_spanB(b, sp, own=(b == own_b), idx=i)
        self.close_scope(self.scopeB)
        scP = []
        self.pmsb = self.sb("pmsb", [16, NB], F32, scope=scP)
        self.ebs = self.sb("ebs", [16, 1], F32, scope=scP)
        mk.op("sync", lambda ss: ss.dma_start(out=self.pmsb[:], in_=self.prefmask),
              writes=["pmsb"], dma=True)
        mk.op("sync", lambda ss: ss.dma_start(out=self.ebs[:], in_=self.ebase),
              writes=["ebs"], dma=True)
        tmp = self.sb("s2tmp", [16, NB], F32, scope=scP)
        mk.op("vector", lambda v: v.tensor_mul(tmp[:], self.cnt_sb[:], self.pmsb[:]),
              reads=["cnt", "pmsb"], writes=["s2tmp"])
        mk.op("vector", lambda v: v.reduce_sum(self.offs[:], tmp[:], axis=AX.X),
              reads=["s2tmp"], writes=["offs"])
        BL = 512
        locb = self.sb("locb", [16, BL], F32, scope=scP)
        glb = self.sb("glb", [16, BL], F32, scope=scP)
        vab = self.sb("vab", [16, BL], F32, scope=scP)
        carry = self.sb("carry", [16, 1], F32, scope=scP)
        prowb = self.sb("prowb", [1, BL], F32, scope=scP)
        prowbi = self.sb("prowbi", [1, BL], I32, scope=scP)
        vrowb = self.sb("vrowb", [1, BL], F32, scope=scP)
        self.pbufh = self.dram("pbufh", [T], I32)
        self.vbufh = self.dram("vbufh", [T], F32)
        mk.op("vector", lambda v: v.memset(carry[:], 0.0), writes=["carry"])
        for blk in range(T // BL):
            c0 = blk * BL
            ohs = self.ohe[0:16, c0:c0 + BL]
            mk.op("vector", lambda v, ohs=ohs: v.tensor_tensor_scan(
                locb[:], ohs, ohs, 0.0, ALU.add, ALU.bypass),
                reads=["ohe"], writes=["locb"], tag=f"scan{blk}")
            mk.op("vector", lambda v, ohs=ohs: v.tensor_sub(locb[:], locb[:], ohs),
                  reads=["locb", "ohe"], writes=["locb"])
            mk.op("vector", lambda v: v.tensor_scalar_add(locb[:], locb[:], carry[:, 0:1]),
                  reads=["locb", "carry"], writes=["locb"])
            mk.op("vector", lambda v, ohs=ohs: v.tensor_add(carry[:, 0:1],
                                                            locb[0:16, BL - 1:BL],
                                                            ohs[0:16, BL - 1:BL]),
                  reads=["locb", "ohe"], writes=["carry"])
            mk.op("vector", lambda v, c0=c0: v.tensor_copy(self.locx[0:16, c0:c0 + BL], locb[:]),
                  reads=["locb"], writes=["locx"])
            mk.op("vector", lambda v: v.tensor_scalar_add(glb[:], locb[:], self.offs[0:16, 0:1]),
                  reads=["locb", "offs"], writes=["glb"])
            mk.op("vector", lambda v: v.tensor_single_scalar(vab[:], glb[:], 2048.0, ALU.is_lt),
                  reads=["glb"], writes=["vab"])
            mk.op("vector", lambda v: v.tensor_single_scalar(glb[:], locb[:], float(CL - 1), ALU.min),
                  reads=["locb"], writes=["glb"])
            mk.op("vector", lambda v: v.tensor_scalar_add(glb[:], glb[:], self.ebs[0:16, 0:1]),
                  reads=["glb", "ebs"], writes=["glb"])
            mk.op("vector", lambda v: v.tensor_single_scalar(glb[:], glb[:], float(TRASH), ALU.subtract),
                  reads=["glb"], writes=["glb"])
            mk.op("vector", lambda v: v.tensor_mul(glb[:], glb[:], vab[:]),
                  reads=["glb", "vab"], writes=["glb"])
            mk.op("vector", lambda v: v.tensor_single_scalar(glb[:], glb[:], float(TRASH), ALU.add),
                  reads=["glb"], writes=["glb"])
            mk.op("vector", lambda v, ohs=ohs: v.tensor_mul(glb[:], glb[:], ohs),
                  reads=["glb", "ohe"], writes=["glb"])
            mk.op("tensor", lambda t: t.matmul(self.PS_C[0:1, 0:BL], self.onesf[:], glb[:],
                                               start=True, stop=True),
                  reads=["glb", "onesf"], writes=["psC"])
            mk.op("vector", lambda v: v.tensor_copy(prowb[:], self.PS_C[0:1, 0:BL]),
                  reads=["psC"], writes=["prowb"])
            mk.op("vector", lambda v, ohs=ohs: v.tensor_mul(vab[:], vab[:], ohs),
                  reads=["vab", "ohe"], writes=["vab"])
            mk.op("tensor", lambda t: t.matmul(self.PS_C[0:1, 0:BL], self.onesf[:], vab[:],
                                               start=True, stop=True),
                  reads=["vab", "onesf"], writes=["psC"])
            mk.op("vector", lambda v: v.tensor_copy(vrowb[:], self.PS_C[0:1, 0:BL]),
                  reads=["psC"], writes=["vrowb"])
            mk.op("vector", lambda v: v.tensor_copy(prowbi[:], prowb[:]),
                  reads=["prowb"], writes=["prowbi"])
            mk.op("sync", lambda ss, c0=c0: ss.dma_start(out=self.pbufh[None, c0:c0 + BL],
                                                         in_=prowbi[:]),
                  reads=["prowbi"], writes=["pbufh"], dma=True)
            mk.op("sync", lambda ss, c0=c0: ss.dma_start(out=self.vbufh[None, c0:c0 + BL],
                                                         in_=vrowb[:]),
                  reads=["vrowb"], writes=["vbufh"], dma=True)
        self.close_scope(scP)
        self.close_scope(self.scopeOwn)

    def _s2_spanA(self, b, sp, idx):
        mk = self.mk
        t0 = sp * SPAN
        lo = max(0, t0 - HALO)
        hi = min(T, t0 + SPAN + HALO)
        c0 = lo - (t0 - HALO)
        c1 = hi - (t0 - HALO)
        P = idx % 2
        xwb, xnw = self.xwbp[P], self.xnwp[P]
        for c in range(NCH):
            if c0 > 0:
                mk.op("vector", lambda v, c=c: v.memset(xwb[:, c * SW:c * SW + c0], 0.0),
                      writes=[("xwb", P)])
            if c1 < SW:
                mk.op("vector", lambda v, c=c: v.memset(xwb[:, c * SW + c1:(c + 1) * SW], 0.0),
                      writes=[("xwb", P)])
            mk.op("gpsimd", lambda g, c=c: g.dma_start(
                out=xwb[:, c * SW + c0:c * SW + c1],
                in_=self.xall[b, 128 * c:128 * c + 128, lo:hi]),
                writes=[("xwb", P)], dma=True, tag=f"ldxwb b{b}s{sp}")
        def vsq(v):
            for c in range(NCH):
                li = v.tensor_mul(self.sqt[:, c * SW:(c + 1) * SW], xwb[:, c * SW:(c + 1) * SW],
                                  xwb[:, c * SW:(c + 1) * SW])
            return li
        mk.op("vector", vsq, reads=[("xwb", P)], writes=["sqt"], lp=True)
        def mm_stats(t):
            for c in range(NCH):
                last = (c == NCH - 1)
                t.matmul(self.PS_A[:, 0:512], self.onesb[:], xwb[:, c * SW:c * SW + 512],
                         start=(c == 0), stop=last)
                t.matmul(self.PS_A[:, 512:516], self.onesb[:], xwb[:, c * SW + 512:(c + 1) * SW],
                         start=(c == 0), stop=last)
                t.matmul(self.PS_B[:, 0:512], self.onesb[:], self.sqt[:, c * SW:c * SW + 512],
                         start=(c == 0), stop=last)
                li = t.matmul(self.PS_B[:, 512:516], self.onesb[:], self.sqt[:, c * SW + 512:(c + 1) * SW],
                              start=(c == 0), stop=last)
            return li
        mk.op("tensor", mm_stats, reads=[("xwb", P), "sqt", "ones"], writes=["psA", "psB"],
              tag=f"r.stats b{b}s{sp}")
        mk.op("scalar", lambda sc: sc.mul(self.mu_t[:], self.PS_A[:, 0:SW], 1.0 / D),
              reads=["psA"], writes=["mu"], lp=True)
        mk.op("vector", lambda v: v.tensor_mul(self.var_t[:], self.mu_t[:], self.mu_t[:]),
              reads=["mu"], writes=["var"])
        mk.op("vector", lambda v: v.scalar_tensor_tensor(self.var_t[:], self.PS_B[:, 0:SW],
                                                         1.0 / D, self.var_t[:],
                                                         ALU.mult, ALU.subtract),
              reads=["psB", "var"], writes=["var"])
        mk.op("scalar", lambda sc: sc.activation(self.var_t[:], self.var_t[:], ACT.Sqrt,
                                                 bias=self.epsv[:, 0:1]),
              reads=["var", "epsv"], writes=["var"])
        mk.op("vector", lambda v: v.reciprocal(self.inv_t[:], self.var_t[:]),
              reads=["var"], writes=["inv"], lp=True)
        for c in range(NCH):
            pass
        def vsub(v):
            for c in range(NCH):
                li = v.tensor_sub(xnw[:, c * SW:(c + 1) * SW],
                                  xwb[:, c * SW:(c + 1) * SW], self.mu_t[:])
            return li
        mk.op("vector", vsub, reads=["mu", ("xwb", P)], writes=[("xnw", P)])
        def vmul(v):
            for c in range(NCH):
                li = v.tensor_mul(xnw[:, c * SW:(c + 1) * SW],
                                  xnw[:, c * SW:(c + 1) * SW], self.inv_t[:])
            return li
        mk.op("vector", vmul, reads=["inv", ("xnw", P)], writes=[("xnw", P)])

    def _s2_spanB(self, b, sp, own, idx):
        mk = self.mk
        t0 = sp * SPAN
        P = idx % 2
        xwb, xnw = self.xwbp[P], self.xnwp[P]
        hbuf_ = self.hbufp[P]
        for fc in range(NCH):
            PS = self.PS_A if fc % 2 == 0 else self.PS_B
            pk = "psA" if fc % 2 == 0 else "psB"
            def mmh(t, fc=fc, PS=PS):
                for gc in range(NCH):
                    last = (gc == NCH - 1)
                    t.matmul(PS[:, 0:512], self.BTs[:, gc * D + fc * 128: gc * D + fc * 128 + 128],
                             xnw[:, gc * SW:gc * SW + 512], start=(gc == 0), stop=last)
                    li = t.matmul(PS[:, 512:516], self.BTs[:, gc * D + fc * 128: gc * D + fc * 128 + 128],
                                  xnw[:, gc * SW + 512:(gc + 1) * SW], start=(gc == 0), stop=last)
                return li
            mk.op("tensor", mmh, reads=[("xnw", P)] + [("bts", gc) for gc in range(NCH)],
                  writes=[pk], tag=f"H b{b}s{sp}f{fc}")
            mk.op("vector", lambda v, fc=fc, PS=PS: v.tensor_copy(
                hbuf_[:, fc * SW:(fc + 1) * SW], PS[:, 0:SW]),
                reads=[pk], writes=[("hbuf", P)], lp=True)
        def mmd2(t):
            for fc in range(NCH):
                last = (fc == NCH - 1)
                t.matmul(self.PS_A[0:1, 0:512], self.u2b[:, fc:fc + 1], xnw[:, fc * SW:fc * SW + 512],
                         start=(fc == 0), stop=last)
                li = t.matmul(self.PS_A[0:1, 512:516], self.u2b[:, fc:fc + 1],
                              xnw[:, fc * SW + 512:(fc + 1) * SW], start=(fc == 0), stop=last)
            return li
        mk.op("tensor", mmd2, reads=[("xnw", P), "u2b"], writes=["psA"], tag="d2")
        mk.op("vector", lambda v: v.tensor_copy(self.d2b[0:1, :], self.PS_A[0:1, 0:SW]),
              reads=["psA"], writes=["d2b"], lp=True)
        for r in range(5):
            cols = min(128, SW - r * 128)
            def mmxv(t, r=r, cols=cols):
                for fc in range(NCH):
                    last = (fc == NCH - 1)
                    t.matmul(self.PS_E[0:cols, r * 16:(r + 1) * 16],
                             xnw[:, fc * SW + r * 128: fc * SW + r * 128 + cols],
                             self.wvg[:, fc * 16:(fc + 1) * 16], start=(fc == 0), stop=last)
                    li = t.matmul(self.PS_D[0:cols, 4:5],
                                  xwb[:, fc * SW + r * 128: fc * SW + r * 128 + cols],
                                  self.wvmb[:, fc:fc + 1], start=(fc == 0), stop=last)
                return li
            mk.op("tensor", mmxv, reads=[("xnw", P), ("xwb", P), "wvg", "wvmb"],
                  writes=["psE", "psD", "psE1"], tag=f"xv b{b}s{sp}r{r}")
            def vxv(v, r=r, cols=cols):
                v.tensor_copy(self.xvbuf[0:cols, r * 16:(r + 1) * 16],
                              self.PS_E[0:cols, r * 16:(r + 1) * 16])
                return v.tensor_copy(self.xwvmcb[0:cols, r:r + 1], self.PS_D[0:cols, 4:5])
            mk.op("vector", vxv, reads=["psE", "psD"], writes=["xvbuf", "xwvmcb"], lp=True)
        def tile_phases(tt):
            gt = sp * 4 + tt
            TP = gt % 2
            ptile_, ptb_, ptT_, ptT2_, rowmx_ = (self.ptilep[TP], self.ptbp[TP],
                                                 self.ptTp[TP], self.ptT2p[TP], self.rowmxp[TP])
            ztile_, negmu_ = self.ztilep[TP], self.negmup[TP]
            m1col_ = self.m1colp[TP]
            bm = 0 if gt == 0 else (2 if gt == 31 else 1)
            w0 = tt * 128
            if TP == 0:
                PS = self.PS_C
                K = "psC0"
                SC0, MU0, Z0, CN0 = 0, 388, 400, 416
                T0, T20, OH0 = 0, 128, 256
            else:
                PS = self.PS_E
                K = "psE1"
                SC0, MU0, Z0, CN0 = 256, 388, 400, 416
                T0, T20, OH0 = 512, 640, 768
            ztk, muk, m1k = f"ztile{TP}", f"negmu{TP}", f"m1col{TP}"
            ph = []
            def mmsc(t):
                for c in range(NCH):
                    t.matmul(PS[:, SC0:SC0 + 132],
                             xnw[:, c * SW + HALO + tt * 128: c * SW + HALO + tt * 128 + 128],
                             hbuf_[:, c * SW + w0: c * SW + w0 + 132],
                             start=(c == 0), stop=False)
                return t.matmul(PS[:, SC0:SC0 + 132], self.onesb[0:1, :],
                                self.d2b[0:1, w0:w0 + 132], start=False, stop=True)
            ph.append(lambda: mk.op("tensor", mmsc,
                                    reads=[("xnw", P), ("hbuf", P), "d2b", "ones",
                                           "psC" if TP == 0 else "psE"],
                                    writes=[K], tag=f"r.score b{b}s{sp}t{tt}"))
            ph.append(lambda: mk.op("vector", lambda v: v.scalar_tensor_tensor(
                ptile_[:], PS[:, SC0:SC0 + 132], 1.0 / 32.0,
                self.band[:, bm * 132:(bm + 1) * 132], ALU.mult, ALU.add),
                reads=[K, "band"], writes=[("ptile", TP)]))
            ph.append(lambda: mk.op("vector", lambda v: v.reduce_max(
                rowmx_[:, 0:1], ptile_[:], axis=AX.X, negate=True),
                reads=[("ptile", TP)], writes=[("rowmx", TP)]))
            ph.append(lambda: mk.op("scalar", lambda sc: sc.activation(
                ptile_[:], ptile_[:], ACT.Exp, bias=rowmx_[:, 0:1], accum_out=rowmx_[:, 1:2]),
                reads=[("ptile", TP), ("rowmx", TP)], writes=[("ptile", TP), ("rowmx2", TP)]))
            ph.append(lambda: mk.op("vector", lambda v: v.reciprocal(
                rowmx_[:, 2:3], rowmx_[:, 1:2]),
                reads=[("rowmx2", TP)], writes=[("rowmx3", TP)]))
            ph.append(lambda: mk.op("vector", lambda v: v.tensor_scalar_mul(
                ptb_[:], ptile_[:], rowmx_[:, 2:3]),
                reads=[("ptile", TP), ("rowmx3", TP)], writes=[("ptb", TP)], lp=True))
            def mmT(t):
                t.transpose(self.PS_T[:, T0:T0 + 128], ptb_[:, 0:128], self.ident[:])
                return t.transpose(self.PS_T[0:4, T20:T20 + 128], ptb_[:, 128:132],
                                   self.ident[:])
            ph.append(lambda: mk.op("tensor", mmT, reads=[("ptb", TP), "ident"],
                                    writes=["psT"], lp=True))
            def vT(v):
                v.tensor_copy(ptT_[:], self.PS_T[:, T0:T0 + 128])
                return v.tensor_copy(ptT2_[:], self.PS_T[0:4, T20:T20 + 128])
            ph.append(lambda: mk.op("vector", vT, reads=["psT"],
                                    writes=[("ptT", TP), ("ptT2", TP)]))
            def mmmu(t):
                for fc in range(NCH):
                    t.matmul(PS[:, MU0:MU0 + 1],
                             xwb[:, fc * SW + HALO + tt * 128: fc * SW + HALO + tt * 128 + 128],
                             self.oodb[:, 0:1], start=(fc == 0), stop=False)
                t.matmul(PS[:, MU0:MU0 + 1], ptT_[:], self.xwvmcb[:, tt:tt + 1],
                         start=False, stop=False)
                return t.matmul(PS[:, MU0:MU0 + 1], ptT2_[:], self.xwvmcb[0:4, tt + 1:tt + 2],
                                start=False, stop=True)
            ph.append(lambda: mk.op("tensor", mmmu,
                                    reads=[("xwb", P), "oodb", ("ptT", TP), ("ptT2", TP),
                                           "xwvmcb"],
                                    writes=[K], tag="negmu"))
            ph.append(lambda: mk.op("scalar", lambda sc: sc.activation(
                negmu_[:], PS[:, MU0:MU0 + 1], ACT.Identity,
                scale=-1.0, bias=self.negbvm[:, 0:1]),
                reads=[K, "negbvm"], writes=[muk]))
            def mmz(t):
                for fc in range(NCH):
                    t.matmul(PS[:, Z0:Z0 + 16],
                             xwb[:, fc * SW + HALO + tt * 128: fc * SW + HALO + tt * 128 + 128],
                             self.wg_sb[:, fc * 16:(fc + 1) * 16], start=(fc == 0), stop=False)
                t.matmul(PS[:, Z0:Z0 + 16], ptT_[:], self.xvbuf[:, tt * 16:(tt + 1) * 16],
                         start=False, stop=False)
                t.matmul(PS[:, Z0:Z0 + 16], ptT2_[:],
                         self.xvbuf[0:4, (tt + 1) * 16:(tt + 2) * 16],
                         start=False, stop=False)
                return t.matmul(PS[:, Z0:Z0 + 16], self.onesb[0:1, :],
                                self.bvgrow[0:1, 0:16], start=False, stop=True)
            ph.append(lambda: mk.op("tensor", mmz,
                                    reads=[("xwb", P), ("ptT", TP), ("ptT2", TP), "xvbuf",
                                           "bvgrow", "ones"]
                                    + [("wg", fc2) for fc2 in range(NCH)],
                                    writes=[K], tag="z"))
            ph.append(lambda: mk.op("vector", lambda v: v.tensor_copy(
                ztile_[:], PS[:, Z0:Z0 + 16]), reads=[K], writes=[ztk]))
            ph.append(lambda: mk.op("vector", lambda v: v.scalar_tensor_tensor(
                ztile_[:], self.gcsb[:], negmu_[:, 0:1], ztile_[:], ALU.mult, ALU.add),
                reads=["gcsb", muk, ztk], writes=[ztk]))
            ph.append(lambda: mk.op("vector", lambda v: v.reduce_max(
                m1col_[:], ztile_[:], axis=AX.X), reads=[ztk], writes=[m1k]))
            oh_dst = self.ohtok[:, gt * 16:(gt + 1) * 16] if own else self.ohtmpp[TP][:]
            ohk = "ohtok" if own else f"ohtmp{TP}"
            ph.append(lambda: mk.op("vector", lambda v: v.tensor_scalar(
                oh_dst, ztile_[:], m1col_[:, 0:1], None, ALU.is_ge),
                reads=[ztk, m1k], writes=[ohk], lp=True))
            def mmcnt(t):
                return t.matmul(PS[0:16, CN0:CN0 + 1], oh_dst, self.onesb[:, 0:1],
                                start=True, stop=True)
            ph.append(lambda: mk.op("tensor", mmcnt, reads=[ohk, "ones"],
                                    writes=[K], tag="cnt"))
            ph.append(lambda: mk.op("vector", lambda v: v.tensor_add(
                self.cnt_sb[0:16, b:b + 1], self.cnt_sb[0:16, b:b + 1],
                PS[0:16, CN0:CN0 + 1]),
                reads=[K, "cnt"], writes=["cnt"]))
            if own:
                def mmToh(t):
                    return t.transpose(self.PS_T[0:16, OH0:OH0 + 128],
                                       self.ohtok[:, gt * 16:(gt + 1) * 16], self.ident[:])
                ph.append(lambda: mk.op("tensor", mmToh, reads=["ohtok", "ident"],
                                        writes=["psT"], lp=True))
                ph.append(lambda: mk.op("vector", lambda v: v.tensor_copy(
                    self.ohe[0:16, gt * 128:(gt + 1) * 128],
                    self.PS_T[0:16, OH0:OH0 + 128]),
                    reads=["psT"], writes=["ohe"]))
            return ph
        for t0_ in range(0, SPAN // 128, 2):
            pha = tile_phases(t0_)
            phb = tile_phases(t0_ + 1)
            for fa, fb in zip(pha, phb):
                fa()
                fb()

    def s3_s4(self):
        mk = self.mk
        NEX, SLOTS = 16, 16 * CL + 128
        self.disp = self.dram("disp", [SLOTS, D], BF16)
        self.outsl = self.dram("outsl", [SLOTS, D], BF16)
        sc3 = []
        resw = self.sb("resw3", [128, NCH * SPAN], F32, scope=sc3)
        zt128 = self.sb("zt128", [128, D], BF16, scope=sc3)
        xrtok = self.sb("xrtok", [128, D], BF16, scope=sc3)
        pidx = self.sb("pidx", [128, 32], I32, scope=sc3)
        vcol = self.sb("vcol", [128, 32], F32, scope=sc3)
        lgt = self.sb("lgt", [128, 16], F32, scope=sc3)
        gtmp = self.sb("gtmp", [128, 16], F32, scope=sc3)
        mk.op("vector", lambda v: v.memset(zt128[:], 0.0), writes=["zt128"])
        for r in range((SLOTS + 127) // 128):
            r0 = r * 128
            rows = min(128, SLOTS - r0)
            mk.op("sync", lambda ss, r0=r0, rows=rows: ss.dma_start(
                out=self.disp[r0:r0 + rows, :], in_=zt128[0:rows, :]),
                reads=["zt128"], writes=["disp"], dma=True, tag="zdisp")
        mk.op("sync", lambda ss: ss.dma_start(
            out=self.outsl[16 * CL:SLOTS, :], in_=zt128[:, :]),
            reads=["zt128"], writes=[("outsl", "trash")], dma=True)
        for g in range(32):
            mk.op("sync", lambda ss, g=g: ss.dma_start(
                out=pidx[:, g:g + 1], in_=self.pbufh[128 * g:128 * g + 128][:, None]),
                reads=["pbufh"], writes=["pidx"], dma=True)
            mk.op("sync", lambda ss, g=g: ss.dma_start(
                out=vcol[:, g:g + 1], in_=self.vbufh[128 * g:128 * g + 128][:, None]),
                reads=["vbufh"], writes=["vcol"], dma=True)
        for sp in range(NSPAN):
            t0 = sp * SPAN
            for c in range(NCH):
                mk.op("sync", lambda ss, c=c, t0=t0: ss.dma_start(
                    out=resw[:, c * SPAN:(c + 1) * SPAN],
                    in_=self.resbuf[:, c * T + t0: c * T + t0 + SPAN]),
                    reads=[("resb", sp)], writes=["resw3"], dma=True, tag=f"ldres3 s{sp}")
            def vcast3(v):
                for c in range(NCH):
                    li = v.tensor_copy(self.xnwp[sp % 2][:, c * SW:c * SW + SPAN],
                                       resw[:, c * SPAN:(c + 1) * SPAN])
                return li
            mk.op("vector", vcast3, reads=["resw3"], writes=[("xnw", sp % 2)], lp=True, tag="cast3")
            def vsq3(v):
                for c in range(NCH):
                    li = v.tensor_mul(self.sqt[:, c * SW:c * SW + SPAN],
                                      self.xnwp[sp % 2][:, c * SW:c * SW + SPAN],
                                      self.xnwp[sp % 2][:, c * SW:c * SW + SPAN])
                return li
            mk.op("vector", vsq3, reads=[("xnw", sp % 2)], writes=["sqt"], lp=True)
            def mm_stats3(t):
                for c in range(NCH):
                    last = (c == NCH - 1)
                    t.matmul(self.PS_A[:, 0:512], self.onesb[:], self.xnwp[sp % 2][:, c * SW:c * SW + 512],
                             start=(c == 0), stop=last)
                    li = t.matmul(self.PS_B[:, 0:512], self.onesb[:], self.sqt[:, c * SW:c * SW + 512],
                                  start=(c == 0), stop=last)
                return li
            mk.op("tensor", mm_stats3, reads=[("xnw", sp % 2), "sqt", "ones"], writes=["psA", "psB"],
                  tag=f"stats3 s{sp}")
            mk.op("scalar", lambda sc: sc.mul(self.mu_t[:, 0:SPAN], self.PS_A[:, 0:SPAN], 1.0 / D),
                  reads=["psA"], writes=["mu"], lp=True)
            mk.op("vector", lambda v: v.tensor_mul(self.var_t[:, 0:SPAN], self.mu_t[:, 0:SPAN],
                                                   self.mu_t[:, 0:SPAN]),
                  reads=["mu"], writes=["var"])
            mk.op("vector", lambda v: v.scalar_tensor_tensor(self.var_t[:, 0:SPAN], self.PS_B[:, 0:SPAN],
                                                             1.0 / D, self.var_t[:, 0:SPAN],
                                                             ALU.mult, ALU.subtract),
                  reads=["psB", "var"], writes=["var"])
            mk.op("scalar", lambda sc: sc.activation(self.var_t[:, 0:SPAN], self.var_t[:, 0:SPAN],
                                                     ACT.Sqrt, bias=self.epsv[:, 0:1]),
                  reads=["var", "epsv"], writes=["var"])
            mk.op("vector", lambda v: v.reciprocal(self.inv_t[:, 0:SPAN], self.var_t[:, 0:SPAN]),
                  reads=["var"], writes=["inv"], lp=True)
            def vsub3(v):
                for c in range(NCH):
                    li = v.tensor_sub(self.xnwp[sp % 2][:, c * SW:c * SW + SPAN],
                                      self.xnwp[sp % 2][:, c * SW:c * SW + SPAN],
                                      self.mu_t[:, 0:SPAN])
                return li
            mk.op("vector", vsub3, reads=["mu", ("xnw", sp % 2)], writes=[("xnw", sp % 2)])
            def vmul3(v):
                for c in range(NCH):
                    li = v.tensor_mul(self.xnwp[sp % 2][:, c * SW:c * SW + SPAN],
                                      self.xnwp[sp % 2][:, c * SW:c * SW + SPAN],
                                      self.inv_t[:, 0:SPAN])
                return li
            mk.op("vector", vmul3, reads=["inv", ("xnw", sp % 2)], writes=[("xnw", sp % 2)])
            for tt in range(SPAN // 128):
                g = sp * 4 + tt
                def mmlg(t, tt=tt):
                    for fc in range(NCH):
                        li = t.matmul(self.PS_E[:, 0:16],
                                      self.xnwp[sp % 2][:, fc * SW + tt * 128: fc * SW + tt * 128 + 128],
                                      self.wg_sb[:, fc * 16:(fc + 1) * 16],
                                      start=(fc == 0), stop=(fc == NCH - 1))
                    return li
                mk.op("tensor", mmlg, reads=[("xnw", sp % 2)] + [("wg", fc) for fc in range(NCH)],
                      writes=["psE"], tag=f"lg s{sp}t{tt}")
                mk.op("vector", lambda v: v.tensor_copy(lgt[:], self.PS_E[:, 0:16]),
                      reads=["psE"], writes=["lgt"])
                mk.op("vector", lambda v: v.reduce_max(self.rowmx[:, 0:1], lgt[:], axis=AX.X,
                                                       negate=True),
                      reads=["lgt"], writes=["rowmx"])
                mk.op("scalar", lambda sc: sc.activation(gtmp[:], lgt[:], ACT.Exp,
                                                         bias=self.rowmx[:, 0:1],
                                                         accum_out=self.rowmx[:, 1:2]),
                      reads=["lgt", "rowmx"], writes=["gtmp", "rowmx2"])
                mk.op("vector", lambda v: v.reciprocal(self.rowmx[:, 2:3], self.rowmx[:, 1:2]),
                      reads=["rowmx2"], writes=["rowmx3"])
                mk.op("vector", lambda v, g=g: v.tensor_mul(gtmp[:], gtmp[:],
                                                            self.ohtok[:, g * 16:(g + 1) * 16]),
                      reads=["gtmp", "ohtok"], writes=["gtmp"])
                mk.op("vector", lambda v: v.reduce_sum(self.rowmx[:, 3:4], gtmp[:], axis=AX.X),
                      reads=["gtmp"], writes=["rowmx4"])
                mk.op("vector", lambda v, g=g: v.tensor_mul(self.gatecol[:, g:g + 1],
                                                            self.rowmx[:, 3:4], self.rowmx[:, 2:3]),
                      reads=["rowmx4", "rowmx3"], writes=["gate"], tag="gsel")
                for fc0 in range(0, NCH, 4):
                    def mmTx(t, tt=tt, fc0=fc0):
                        for j in range(4):
                            li = t.transpose(
                                self.PS_T[:, j * 256:j * 256 + 128],
                                self.xnwp[sp % 2][:, (fc0 + j) * SW + tt * 128:
                                                  (fc0 + j) * SW + tt * 128 + 128],
                                self.ident[:])
                        return li
                    mk.op("tensor", mmTx, reads=[("xnw", sp % 2), "ident"], writes=["psT"], lp=True)
                    def vTx(v, fc0=fc0):
                        for j in range(4):
                            li = v.tensor_copy(xrtok[:, (fc0 + j) * 128:(fc0 + j + 1) * 128],
                                               self.PS_T[:, j * 256:j * 256 + 128])
                        return li
                    mk.op("vector", vTx, reads=["psT"], writes=["xrtok"])
                mk.op("gpsimd", lambda g2, g=g: g2.indirect_dma_start(
                    self.disp, IndirectOffsetOnAxis(pidx[:, g:g + 1], axis=0),
                    xrtok[:], None),
                    reads=["xrtok", "pidx", "disp"], writes=[("dispw", g)], dma=True,
                    tag=f"scatter {g}")
        self.close_scope(sc3)
        sc4 = []
        w1b = [self.sb(f"w1b{i}", [128, 4096], BF16, scope=sc4) for i in range(NCH)]
        dispsb = self.sb("dispsb", [128, 4 * D], BF16, scope=sc4)
        dispT = self.sb("dispT", [128, NCH * 512], BF16, scope=sc4)
        hT = self.sb("hT", [128, 32 * 512], BF16, scope=sc4)
        w2b = self.sb("w2b", [128, 2 * 1024], BF16, scope=sc4)
        out2sb = self.sb("out2sb", [128, 4 * D], BF16, scope=sc4)
        dispkeys = [("dispw", g) for g in range(32)] + ["disp"]
        for e in range(NEX):
            for r in range(4):
                mk.op("sync", lambda ss, e=e, r=r: ss.dma_start(
                    out=dispsb[:, r * D:(r + 1) * D],
                    in_=self.disp[e * CL + r * 128: e * CL + r * 128 + 128, :]),
                    reads=dispkeys, writes=["dispsb"], dma=True, tag=f"lddisp e{e}")
            for r in range(4):
                for fc0 in range(0, NCH, 4):
                    def mmT(t, r=r, fc0=fc0):
                        for j in range(4):
                            li = t.transpose(
                                self.PS_T[:, j * 256:j * 256 + 128],
                                dispsb[:, r * D + (fc0 + j) * 128: r * D + (fc0 + j) * 128 + 128],
                                self.ident[:])
                        return li
                    mk.op("tensor", mmT, reads=["dispsb", "ident"], writes=["psT4"], lp=True)
                    def vT(v, r=r, fc0=fc0):
                        for j in range(4):
                            li = v.tensor_copy(
                                dispT[:, (fc0 + j) * 512 + r * 128: (fc0 + j) * 512 + r * 128 + 128],
                                self.PS_T[:, j * 256:j * 256 + 128])
                        return li
                    mk.op("vector", vT, reads=["psT4"], writes=["dispT"])
            for fc in range(NCH):
                mk.op("gpsimd", lambda g2, e=e, fc=fc: g2.dma_start(
                    out=w1b[fc][:], in_=self.w1[e, 128 * fc:128 * fc + 128, :]),
                    writes=[("w1b", fc)], dma=True, tag=f"ldw1 e{e}f{fc}")
            for Fc in range(32):
                PS = (self.PS_C, "psC") if Fc % 2 == 0 else (self.PS_D, "psD")
                def mmh(t, Fc=Fc, PS=PS[0]):
                    for fc in range(NCH):
                        li = t.matmul(PS[:, 0:512], w1b[fc][:, Fc * 128:(Fc + 1) * 128],
                                      dispT[:, fc * 512:(fc + 1) * 512],
                                      start=(fc == 0), stop=(fc == NCH - 1))
                    return li
                mk.op("tensor", mmh, reads=["dispT"] + [("w1b", fc) for fc in range(NCH)],
                      writes=[PS[1]], tag=f"hT e{e}F{Fc}")
                mk.op("scalar", lambda sc, Fc=Fc, PS=PS[0]: sc.activation(
                    hT[:, Fc * 512:(Fc + 1) * 512], PS[:, 0:512], ACT.Relu),
                    reads=[PS[1]], writes=["hT"], tag="relu")
            for mh in range(2):
                for Fg in range(16):
                    mk.op("gpsimd", lambda g2, e=e, mh=mh, Fg=Fg: g2.dma_start(
                        out=w2b[:, (Fg % 2) * 1024:(Fg % 2) * 1024 + 1024],
                        in_=self.w2[e, mh, Fg]),
                        writes=[("w2b", Fg % 2)], reads=[("w2bc", Fg % 2)], dma=True,
                        tag=f"ldw2 e{e}m{mh}F{Fg}")
                    def mmo(t, mh=mh, Fg=Fg):
                        for k in range(2):
                            Fc = 2 * Fg + k
                            for r in range(4):
                                PS = (self.PS_A if r < 2 else self.PS_B)
                                li = t.matmul(PS[:, (r % 2) * 512:(r % 2) * 512 + 512],
                                              hT[:, Fc * 512 + r * 128: Fc * 512 + r * 128 + 128],
                                              w2b[:, (Fg % 2) * 1024 + k * 512:(Fg % 2) * 1024 + k * 512 + 512],
                                              start=(Fc == 0), stop=(Fc == 31))
                        return li
                    mk.op("tensor", mmo, reads=["hT", ("w2b", Fg % 2)],
                          writes=["psA", "psB", ("w2bc", Fg % 2)], tag=f"out2 e{e}m{mh}F{Fg}")
                for r in range(4):
                    PS = (self.PS_A if r < 2 else self.PS_B)
                    pk = "psA" if r < 2 else "psB"
                    mk.op("vector", lambda v, r=r, mh=mh, PS=PS: v.tensor_copy(
                        out2sb[:, r * D + mh * 512: r * D + mh * 512 + 512],
                        PS[:, (r % 2) * 512:(r % 2) * 512 + 512]),
                        reads=[pk], writes=["out2sb"], lp=True)
            for r in range(4):
                mk.op("sync", lambda ss, e=e, r=r: ss.dma_start(
                    out=self.outsl[e * CL + r * 128: e * CL + r * 128 + 128, :],
                    in_=out2sb[:, r * D:(r + 1) * D]),
                    reads=["out2sb"], writes=[("outsl", e)], dma=True, tag=f"stout e{e}")
        self.close_scope(sc4)
        sc5 = []
        resw5 = self.sb("resw5", [128, NCH * SPAN], F32, scope=sc5)
        comb = self.sb("comb", [128, D], BF16, scope=sc5)
        pidx5 = self.sb("pidx5", [128, 32], I32, scope=sc5)
        scal = self.sb("scal", [128, 32], F32, scope=sc5)
        for g in range(32):
            mk.op("sync", lambda ss, g=g: ss.dma_start(
                out=pidx5[:, g:g + 1], in_=self.pbufh[128 * g:128 * g + 128][:, None]),
                reads=["pbufh"], writes=["pidx5"], dma=True)
            mk.op("sync", lambda ss, g=g: ss.dma_start(
                out=scal[:, g:g + 1], in_=self.vbufh[128 * g:128 * g + 128][:, None]),
                reads=["vbufh"], writes=["scal"], dma=True)
        mk.op("vector", lambda v: v.tensor_mul(scal[:], scal[:], self.gatecol[:]),
              reads=["scal", "gate"], writes=["scal"], tag="scal=v*g")
        outkeys = [("outsl", e) for e in range(NEX)] + [("outsl", "trash")]
        for sp in range(NSPAN):
            t0 = sp * SPAN
            for c in range(NCH):
                mk.op("sync", lambda ss, c=c, t0=t0: ss.dma_start(
                    out=resw5[:, c * SPAN:(c + 1) * SPAN],
                    in_=self.resbuf[:, c * T + t0: c * T + t0 + SPAN]),
                    reads=[("resb", sp)], writes=["resw5"], dma=True)
            for tt in range(SPAN // 128):
                g = sp * 4 + tt
                mk.op("gpsimd", lambda g2, g=g: g2.indirect_dma_start(
                    comb[:], None, self.outsl,
                    IndirectOffsetOnAxis(pidx5[:, g:g + 1], axis=0)),
                    reads=outkeys + ["pidx5"], writes=["comb"], dma=True, tag=f"gather {g}")
                mk.op("vector", lambda v, g=g: v.tensor_scalar_mul(comb[:], comb[:],
                                                                   scal[:, g:g + 1]),
                      reads=["comb", "scal"], writes=["comb"], lp=True)
                for fc in range(NCH):
                    mk.op("tensor", lambda t, fc=fc: t.transpose(
                        self.PS_T[:, 0:128], comb[:, fc * 128:(fc + 1) * 128], self.ident[:]),
                        reads=["comb", "ident"], writes=["psT"], lp=True)
                    mk.op("vector", lambda v, fc=fc, tt=tt: v.tensor_add(
                        resw5[:, fc * SPAN + tt * 128: fc * SPAN + tt * 128 + 128],
                        resw5[:, fc * SPAN + tt * 128: fc * SPAN + tt * 128 + 128],
                        self.PS_T[:, 0:128]),
                        reads=["psT", "resw5"], writes=["resw5"], tag="comb+res")
            for c in range(NCH):
                mk.op("sync", lambda ss, c=c, t0=t0, sp=sp: ss.dma_start(
                    out=self.resbuf[:, c * T + t0: c * T + t0 + SPAN],
                    in_=resw5[:, c * SPAN:(c + 1) * SPAN]),
                    reads=["resw5"], writes=[("resb", sp)], dma=True, tag=f"stres4 s{sp}")
        self.close_scope(sc5)
        self.close_scope(self.scopeXN)

    def s5(self):
        mk = self.mk
        M = 256
        DH = 64
        sc6 = []
        xn2 = self.sb("xn2", [128, NCH * T], BF16, scope=sc6)
        resw = self.sb("resw6", [128, NCH * SPAN], F32, scope=sc6)
        for sp in range(NSPAN):
            t0 = sp * SPAN
            for c in range(NCH):
                mk.op("sync", lambda ss, c=c, t0=t0: ss.dma_start(
                    out=resw[:, c * SPAN:(c + 1) * SPAN],
                    in_=self.resbuf[:, c * T + t0: c * T + t0 + SPAN]),
                    reads=[("resb", sp)], writes=["resw6"], dma=True, tag=f"ldres6 s{sp}")
            for c in range(NCH):
                mk.op("vector", lambda v, c=c, t0=t0: v.tensor_copy(
                    xn2[:, c * T + t0: c * T + t0 + SPAN], resw[:, c * SPAN:(c + 1) * SPAN]),
                    reads=["resw6"], writes=["xn2"], lp=True)
                mk.op("vector", lambda v, c=c, t0=t0: v.tensor_mul(
                    self.sqt[:, 0:SPAN], xn2[:, c * T + t0: c * T + t0 + SPAN],
                    xn2[:, c * T + t0: c * T + t0 + SPAN]),
                    reads=["xn2"], writes=["sqt"], lp=True)
                def mm_stats(t, c=c, t0=t0):
                    last = (c == NCH - 1)
                    t.matmul(self.PS_A[:, 0:512], self.onesb[:],
                             xn2[:, c * T + t0: c * T + t0 + SPAN], start=(c == 0), stop=last)
                    return t.matmul(self.PS_B[:, 0:512], self.onesb[:], self.sqt[:, 0:512],
                                    start=(c == 0), stop=last)
                mk.op("tensor", mm_stats, reads=["xn2", "sqt", "ones"], writes=["psA", "psB"])
            mk.op("scalar", lambda sc: sc.mul(self.mu_t[:, 0:SPAN], self.PS_A[:, 0:SPAN], 1.0 / D),
                  reads=["psA"], writes=["mu"], lp=True)
            mk.op("vector", lambda v: v.tensor_mul(self.var_t[:, 0:SPAN], self.mu_t[:, 0:SPAN],
                                                   self.mu_t[:, 0:SPAN]),
                  reads=["mu"], writes=["var"])
            mk.op("vector", lambda v: v.scalar_tensor_tensor(self.var_t[:, 0:SPAN],
                                                             self.PS_B[:, 0:SPAN], 1.0 / D,
                                                             self.var_t[:, 0:SPAN],
                                                             ALU.mult, ALU.subtract),
                  reads=["psB", "var"], writes=["var"])
            mk.op("scalar", lambda sc: sc.activation(self.var_t[:, 0:SPAN], self.var_t[:, 0:SPAN],
                                                     ACT.Sqrt, bias=self.epsv[:, 0:1]),
                  reads=["var", "epsv"], writes=["var"])
            mk.op("vector", lambda v: v.reciprocal(self.inv_t[:, 0:SPAN], self.var_t[:, 0:SPAN]),
                  reads=["var"], writes=["inv"], lp=True)
            for c in range(NCH):
                mk.op("vector", lambda v, c=c, t0=t0: v.tensor_sub(
                    xn2[:, c * T + t0: c * T + t0 + SPAN],
                    xn2[:, c * T + t0: c * T + t0 + SPAN], self.mu_t[:, 0:SPAN]),
                    reads=["mu", "xn2"], writes=["xn2"])
                mk.op("vector", lambda v, c=c, t0=t0: v.tensor_mul(
                    xn2[:, c * T + t0: c * T + t0 + SPAN],
                    xn2[:, c * T + t0: c * T + t0 + SPAN], self.inv_t[:, 0:SPAN]),
                    reads=["inv", "xn2"], writes=["xn2"])
        wcb = self.sb("wcb", [128, 208], BF16, scope=sc6)
        wcrow = self.sb("wcrow", [1, 208], BF16, scope=sc6)
        mk.op("gpsimd", lambda g: g.dma_start(out=wcrow[:], in_=self.wconvr),
              writes=["wcrow"], dma=True)
        mk.op("tensor", lambda t: t.matmul(self.PS_A[:, 0:208], self.onesb[0:1, :], wcrow[0:1, :],
                                           start=True, stop=True),
              reads=["wcrow", "ones"], writes=["psA"])
        mk.op("vector", lambda v: v.tensor_copy(wcb[:], self.PS_A[:, 0:208]),
              reads=["psA"], writes=["wcb"], lp=True)
        spc = self.sb("spc", [128, 3 * 13 * 128], BF16, scope=sc6)
        for which, src in ((0, self.scur), (1, self.sprev), (2, self.snext)):
            for j in range(13):
                mk.op("gpsimd", lambda g, which=which, src=src, j=j: g.dma_start(
                    out=spc[:, (which * 13 + j) * 128:(which * 13 + j) * 128 + 128],
                    in_=src[j]),
                    writes=["spc"], dma=True)
        id2 = self.sb("id2", [128, 2 * 256], BF16, scope=sc6)
        mk.op("vector", lambda v: v.memset(id2[:], 0.0), writes=["id2"])
        mk.op("vector", lambda v: v.tensor_copy(id2[:, 0:128], self.ident[:]),
              reads=["ident", "id2"], writes=["id2"])
        mk.op("vector", lambda v: v.tensor_copy(id2[:, 256 + 128:256 + 256], self.ident[:]),
              reads=["ident", "id2"], writes=["id2"])
        self.sfmh = self.dram("sfmh", [D, T], BF16)
        self._w8 = {k: self.sb(k, [128, NCH * 128], BF16, scope=sc6) for k in ("wq8", "wk8", "wv8")}
        q2 = self.sb("q2", [128, T], BF16, scope=sc6)
        k2 = self.sb("k2", [128, T], BF16, scope=sc6)
        vtok2 = self.sb("vtok2", [128, 32 * 130], BF16, scope=sc6)
        ql2 = self.sb("ql2", [128, M], BF16, scope=sc6)
        kl2 = self.sb("kl2", [128, M], BF16, scope=sc6)
        wqt = self.sb("wqt", [128, 2 * 128], BF16, scope=sc6)
        sraw = self.sb("sraw", [128, 2 * T], BF16, scope=sc6)
        s3raw = self.sb("s3raw", [128, 32 * M], BF16, scope=sc6)
        a2 = self.sb("a2", [128, 2 * M], BF16, scope=sc6)
        a2T = self.sb("a2T", [128, 2 * M], BF16, scope=sc6)
        zz = self.sb("zz", [128, 2 * M], BF16, scope=sc6)
        zzT = self.sb("zzT", [128, 2 * M], BF16, scope=sc6)
        az = self.sb("az", [128, 2 * M], BF16, scope=sc6)
        azT = self.sb("azT", [128, 2 * M], BF16, scope=sc6)
        tm1 = self.sb("tm1", [128, 2 * M], BF16, scope=sc6)
        tm2 = self.sb("tm2", [128, 2 * M], BF16, scope=sc6)
        yav = self.sb("yav", [128, 2 * 65], BF16, scope=sc6)
        yv = self.sb("yv", [128, 2 * 65], BF16, scope=sc6)
        outk = self.sb("outk", [128, 32 * DH], BF16, scope=sc6)
        restk = self.sb("restk", [128, 32 * DH], BF16, scope=sc6)
        t2cur = self.sb("t2cur", [128, 3 * 128], BF16, scope=sc6)
        sfm2 = self.sb("sfm2", [128, T], BF16, scope=sc6)
        sc1b = self.sb("sc1b", [128, 8], F32, scope=sc6)
        sconst = self.sb("sconst", [128, 4], BF16, scope=sc6)
        for hp in range(8):
            self._s5_pair(hp, xn2, q2, k2, vtok2, ql2, kl2, wqt, sraw, s3raw, a2, a2T,
                          zz, zzT, az, azT, tm1, tm2, yav, yv, outk, restk, t2cur,
                          sfm2, sc1b, sconst, wcb, spc, id2)
        self.close_scope(sc6)
        sc7 = []
        sfmc = self.sb("sfmc", [128, T], BF16, scope=sc7)
        wfct = self.sb("wfct", [128, 2 * 128], BF16, scope=sc7)
        resw7 = self.sb("resw7", [128, T], F32, scope=sc7)
        self._s5_fc(sfmc, wfct, resw7)
        self.close_scope(sc7)

    def _s5_pair(self, hp, xn2, q2, k2, vtok2, ql2, kl2, wqt, sraw, s3raw, a2, a2T,
                 zz, zzT, az, azT, tm1, tm2, yav, yv, outk, restk, t2cur,
                 sfm2, sc1b, sconst, wcb, spc, id2):
        mk = self.mk
        M, DH = 256, 64
        for base, pkey in ((0, "wq8"), (D, "wk8"), (2 * D, "wv8")):
            for ic in range(NCH):
                mk.op("gpsimd", lambda g, ic=ic, base=base, pkey=pkey: g.dma_start(
                    out=wqt[:, 0:128] if False else self._w8[pkey][:, ic * 128:(ic + 1) * 128],
                    in_=self.wqkv[128 * ic:128 * ic + 128, base + hp * 128: base + hp * 128 + 128]),
                    writes=[(pkey, ic)], reads=[(pkey + "c", ic)], dma=True, tag=f"ld{pkey} hp{hp}")
        wq8, wk8, wv8 = self._w8["wq8"], self._w8["wk8"], self._w8["wv8"]
        for dst, w8, wkey, key, scale in ((q2, wq8, "wq8", "q2", 0.125), (k2, wk8, "wk8", "k2", 1.0)):
            for ns in range(4):
                PS = self.PS_A if ns % 2 == 0 else self.PS_B
                pk = "psA" if ns % 2 == 0 else "psB"
                def mmq(t, ns=ns, PS=PS, w8=w8):
                    for ic in range(NCH):
                        last = (ic == NCH - 1)
                        t.matmul(PS[:, 0:512], w8[:, ic * 128:(ic + 1) * 128],
                                 xn2[:, ic * T + ns * 1024: ic * T + ns * 1024 + 512],
                                 start=(ic == 0), stop=last)
                        li = t.matmul(PS[:, 512:1024], w8[:, ic * 128:(ic + 1) * 128],
                                      xn2[:, ic * T + ns * 1024 + 512: ic * T + (ns + 1) * 1024],
                                      start=(ic == 0), stop=last)
                    return li
                mk.op("tensor", mmq, reads=["xn2"] + [(wkey, ic) for ic in range(NCH)],
                      writes=[pk], tag=f"qk hp{hp}n{ns}")
                mk.op("scalar", lambda sc, ns=ns, dst=dst, scale=scale, PS=PS: sc.activation(
                    dst[:, ns * 1024:(ns + 1) * 1024], PS[:, 0:1024], ACT.Identity, scale=scale),
                    reads=[pk], writes=[key, wkey + "c0"], lp=True)
            mk.op("vector", lambda v, wkey=wkey: v.engine_nop(),
                  reads=[wkey + "c0"], writes=[(wkey + "c", ic2) for ic2 in range(NCH)])
        mk.op("vector", lambda v: v.memset(
            vtok2.rearrange("p (r c) -> p r c", c=130)[:, :, 0:1], 1.0),
            writes=["vtok2i"], tag="vtok ones0")
        mk.op("vector", lambda v: v.memset(
            vtok2.rearrange("p (r c) -> p r c", c=130)[:, :, 129:130], 1.0),
            writes=["vtok2i"], tag="vtok ones1")
        for r in range(32):
            PS = self.PS_A if r % 2 == 0 else self.PS_B
            pk = "psA" if r % 2 == 0 else "psB"
            def mmv(t, r=r, PS=PS):
                for ic in range(NCH):
                    li = t.matmul(PS[:, 0:128], xn2[:, ic * T + r * 128: ic * T + r * 128 + 128],
                                  wv8[:, ic * 128:(ic + 1) * 128],
                                  start=(ic == 0), stop=(ic == NCH - 1))
                return li
            mk.op("tensor", mmv, reads=["xn2"] + [("wv8", ic) for ic in range(NCH)],
                  writes=[pk], tag=f"vt hp{hp}r{r}")
            mk.op("vector", lambda v, r=r, PS=PS: v.tensor_copy(
                vtok2[:, r * 130 + 1: r * 130 + 129], PS[:, 0:128]),
                reads=[pk, "vtok2i"], writes=["vtok2", "wv8c0"], lp=True)
        mk.op("vector", lambda v: v.engine_nop(),
              reads=["wv8c0"], writes=[("wv8c", ic2) for ic2 in range(NCH)])
        mk.op("vector", lambda v: v.reduce_sum(ql2[:], q2.rearrange("p (m l) -> p m l", l=16),
                                               axis=AX.X),
              reads=["q2"], writes=["ql2"], lp=True)
        mk.op("vector", lambda v: v.tensor_single_scalar(ql2[:], ql2[:], 1.0 / 16, ALU.mult),
              reads=["ql2"], writes=["ql2"], lp=True)
        mk.op("vector", lambda v: v.reduce_sum(kl2[:], k2.rearrange("p (m l) -> p m l", l=16),
                                               axis=AX.X),
              reads=["k2"], writes=["kl2"], lp=True)
        mk.op("vector", lambda v: v.tensor_single_scalar(kl2[:], kl2[:], 1.0 / 16, ALU.mult),
              reads=["kl2"], writes=["kl2"], lp=True)
        for hh in range(2):
            d0 = hh * 64
            h = 2 * hp + hh
            for mc in range(2):
                mk.op("tensor", lambda t, mc=mc, d0=d0: t.matmul(
                    self.PS_C[:, 0:256], ql2[d0:d0 + 64, mc * 128:(mc + 1) * 128],
                    kl2[d0:d0 + 64, :], start=True, stop=True),
                    reads=["ql2", "kl2"], writes=["psC"], tag=f"sim2 h{h}m{mc}")
                mk.op("vector", lambda v: v.reduce_max(self.rowmx[:, 0:1], self.PS_C[:, 0:256],
                                                       axis=AX.X, negate=True),
                      reads=["psC"], writes=["rowmx"])
                mk.op("scalar", lambda sc, mc=mc: sc.activation(
                    a2[:, mc * 256:(mc + 1) * 256], self.PS_C[:, 0:256], ACT.Exp,
                    bias=self.rowmx[:, 0:1], accum_out=self.rowmx[:, 1:2]),
                    reads=["psC", "rowmx"], writes=["a2x", "rowmx2"], lp=True)
                mk.op("vector", lambda v: v.reciprocal(self.rowmx[:, 2:3], self.rowmx[:, 1:2]),
                      reads=["rowmx2"], writes=["rowmx3"])
                mk.op("vector", lambda v, mc=mc: v.tensor_scalar_mul(
                    a2[:, mc * 256:(mc + 1) * 256], a2[:, mc * 256:(mc + 1) * 256],
                    self.rowmx[:, 2:3]),
                    reads=["a2x", "rowmx3"], writes=["a2"], lp=True)
            for ii in range(2):
                def mmTa(t, ii=ii):
                    t.transpose(self.PS_T[:, 0:128],
                                a2[:, ii * 256: ii * 256 + 128], self.ident[:])
                    return t.transpose(self.PS_T[:, 128:256],
                                       a2[:, ii * 256 + 128: ii * 256 + 256], self.ident[:])
                mk.op("tensor", mmTa, reads=["a2"], writes=["psT"], lp=True)
                def vTa(v, ii=ii):
                    v.tensor_copy(a2T[:, ii * 128: ii * 128 + 128], self.PS_T[:, 0:128])
                    return v.tensor_copy(a2T[:, 256 + ii * 128: 256 + ii * 128 + 128],
                                         self.PS_T[:, 128:256])
                mk.op("vector", vTa, reads=["psT"], writes=["a2T"])
            def mmcs(t):
                for mc in range(2):
                    li = t.matmul(self.PS_C[0:1, 256:512], self.onesb[:, 0:1],
                                  a2[:, mc * 256:(mc + 1) * 256],
                                  start=(mc == 0), stop=(mc == 1))
                return li
            mk.op("tensor", mmcs, reads=["a2", "ones"], writes=["psC"], tag="colsum")
            mk.op("vector", lambda v: v.reduce_max(self.rowmx[0:1, 0:1], self.PS_C[0:1, 256:512],
                                                   axis=AX.X),
                  reads=["psC"], writes=["mrv"])
            mk.op("vector", lambda v: v.reciprocal(self.rowmx[0:1, 1:2], self.rowmx[0:1, 0:1]),
                  reads=["mrv"], writes=["mri"])
            mk.op("vector", lambda v: v.tensor_copy(sconst[0:1, 0:1], self.rowmx[0:1, 1:2]),
                  reads=["mri"], writes=["mrib"], lp=True)
            mk.op("tensor", lambda t: t.matmul(self.PS_C[:, 512 - 1:512], self.onesb[0:1, :],
                                               sconst[0:1, 0:1], start=True, stop=True),
                  reads=["mrib", "ones"], writes=["psC"])
            mk.op("vector", lambda v: v.tensor_copy(sc1b[:, 0:1], self.PS_C[:, 511:512]),
                  reads=["psC"], writes=["mribc"])
            mk.op("vector", lambda v: v.tensor_scalar_mul(zz[:], a2T[:], sc1b[:, 0:1]),
                  reads=["a2T", "mribc"], writes=["zz"], lp=True)
            mk.op("vector", lambda v: v.tensor_scalar_mul(zzT[:], a2[:], sc1b[:, 0:1]),
                  reads=["a2", "mribc"], writes=["zzT"], lp=True)
            for it in range(6):
                for oc2 in range(2):
                    def mmaz(t, oc2=oc2):
                        for kc2 in range(2):
                            li = t.matmul(self.PS_C[:, 0:256],
                                          a2T[:, kc2 * 256 + oc2 * 128: kc2 * 256 + oc2 * 128 + 128],
                                          zz[:, kc2 * 256:(kc2 + 1) * 256],
                                          start=(kc2 == 0), stop=(kc2 == 1))
                        return li
                    mk.op("tensor", mmaz, reads=["a2T", "zz"], writes=["psC"], tag=f"az{it}")
                    mk.op("vector", lambda v, oc2=oc2: v.tensor_copy(
                        az[:, oc2 * 256:(oc2 + 1) * 256], self.PS_C[:, 0:256]),
                        reads=["psC"], writes=["az"], lp=True)
                for ii in range(2):
                    def mmTz(t, ii=ii):
                        t.transpose(self.PS_T[:, 0:128],
                                    az[:, ii * 256: ii * 256 + 128], self.ident[:])
                        return t.transpose(self.PS_T[:, 128:256],
                                           az[:, ii * 256 + 128: ii * 256 + 256], self.ident[:])
                    mk.op("tensor", mmTz, reads=["az"], writes=["psT"], lp=True)
                    def vTz(v, ii=ii):
                        v.tensor_copy(azT[:, ii * 128: ii * 128 + 128], self.PS_T[:, 0:128])
                        return v.tensor_copy(azT[:, 256 + ii * 128: 256 + ii * 128 + 128],
                                             self.PS_T[:, 128:256])
                    mk.op("vector", vTz, reads=["psT"], writes=["azT"])
                mk.op("vector", lambda v: v.scalar_tensor_tensor(tm1[:], id2[:], 7.0, az[:],
                                                                 ALU.mult, ALU.subtract),
                      reads=["id2", "az"], writes=["tm1"], lp=True)
                for oc2 in range(2):
                    def mmt2(t, oc2=oc2):
                        for kc2 in range(2):
                            li = t.matmul(self.PS_C[:, 0:256],
                                          azT[:, kc2 * 256 + oc2 * 128: kc2 * 256 + oc2 * 128 + 128],
                                          tm1[:, kc2 * 256:(kc2 + 1) * 256],
                                          start=(kc2 == 0), stop=(kc2 == 1))
                        return li
                    mk.op("tensor", mmt2, reads=["azT", "tm1"], writes=["psC"], tag=f"azt1_{it}")
                    mk.op("vector", lambda v, oc2=oc2: v.tensor_copy(
                        tm2[:, oc2 * 256:(oc2 + 1) * 256], self.PS_C[:, 0:256]),
                        reads=["psC"], writes=["tm2"], lp=True)
                mk.op("vector", lambda v: v.scalar_tensor_tensor(tm1[:], id2[:], 15.0, tm2[:],
                                                                 ALU.mult, ALU.subtract),
                      reads=["id2", "tm2"], writes=["tm1"], lp=True)
                for oc2 in range(2):
                    def mmt3(t, oc2=oc2):
                        for kc2 in range(2):
                            li = t.matmul(self.PS_C[:, 0:256],
                                          azT[:, kc2 * 256 + oc2 * 128: kc2 * 256 + oc2 * 128 + 128],
                                          tm1[:, kc2 * 256:(kc2 + 1) * 256],
                                          start=(kc2 == 0), stop=(kc2 == 1))
                        return li
                    mk.op("tensor", mmt3, reads=["azT", "tm1"], writes=["psC"], tag=f"azt2_{it}")
                    mk.op("vector", lambda v, oc2=oc2: v.tensor_copy(
                        tm2[:, oc2 * 256:(oc2 + 1) * 256], self.PS_C[:, 0:256]),
                        reads=["psC"], writes=["tm2"], lp=True)
                mk.op("vector", lambda v: v.scalar_tensor_tensor(tm1[:], id2[:], 13.0, tm2[:],
                                                                 ALU.mult, ALU.subtract),
                      reads=["id2", "tm2"], writes=["tm1"], lp=True)
                for oc2 in range(2):
                    def mmzu(t, oc2=oc2):
                        for kc2 in range(2):
                            li = t.matmul(self.PS_C[:, 0:256],
                                          zzT[:, kc2 * 256 + oc2 * 128: kc2 * 256 + oc2 * 128 + 128],
                                          tm1[:, kc2 * 256:(kc2 + 1) * 256],
                                          start=(kc2 == 0), stop=(kc2 == 1))
                        return li
                    mk.op("tensor", mmzu, reads=["zzT", "tm1"], writes=["psC"], tag=f"zup{it}")
                    mk.op("scalar", lambda sc, oc2=oc2: sc.activation(
                        zz[:, oc2 * 256:(oc2 + 1) * 256], self.PS_C[:, 0:256], ACT.Identity,
                        scale=0.25),
                        reads=["psC"], writes=["zz"], lp=True)
                for ii in range(2):
                    def mmTw(t, ii=ii):
                        t.transpose(self.PS_T[:, 0:128],
                                    zz[:, ii * 256: ii * 256 + 128], self.ident[:])
                        return t.transpose(self.PS_T[:, 128:256],
                                           zz[:, ii * 256 + 128: ii * 256 + 256], self.ident[:])
                    mk.op("tensor", mmTw, reads=["zz"], writes=["psT"], lp=True)
                    def vTw(v, ii=ii):
                        v.tensor_copy(zzT[:, ii * 128: ii * 128 + 128], self.PS_T[:, 0:128])
                        return v.tensor_copy(zzT[:, 256 + ii * 128: 256 + ii * 128 + 128],
                                             self.PS_T[:, 128:256])
                    mk.op("vector", vTw, reads=["psT"], writes=["zzT"])
            mk.op("vector", lambda v: v.memset(sc1b[:, 1:2], NEG), writes=["accm3"])
            for r in range(32):
                mk.op("tensor", lambda t, r=r, d0=d0: t.matmul(
                    self.PS_D[:, 0:256], k2[d0:d0 + 64, r * 128:(r + 1) * 128],
                    ql2[d0:d0 + 64, :], start=True, stop=True),
                    reads=["k2", "ql2"], writes=["psD"], tag=f"sim3 h{h}r{r}")
                mk.op("vector", lambda v, r=r: v.tensor_copy(s3raw[:, r * 256:(r + 1) * 256],
                                                             self.PS_D[:, 0:256]),
                      reads=["psD"], writes=["s3raw"], lp=True)
                mk.op("vector", lambda v, r=r: v.reduce_max(self.rowmx[:, 0:1],
                                                            s3raw[:, r * 256:(r + 1) * 256],
                                                            axis=AX.X),
                      reads=["s3raw"], writes=["rowmx"])
                mk.op("vector", lambda v: v.tensor_max(sc1b[:, 1:2], sc1b[:, 1:2],
                                                       self.rowmx[:, 0:1]),
                      reads=["rowmx", "accm3"], writes=["accm3"])
            mk.op("vector", lambda v: v.tensor_copy(sconst[:, 1:2], sc1b[:, 1:2]),
                  reads=["accm3"], writes=["accm3b"], lp=True)
            mk.op("tensor", lambda t: t.transpose(self.PS_T[0:1, 128:256], sconst[:, 1:2],
                                                  self.ident[:]),
                  reads=["accm3b"], writes=["psT2"], lp=True)
            mk.op("vector", lambda v: v.reduce_max(self.rowmx[0:1, 1:2], self.PS_T[0:1, 128:256],
                                                   axis=AX.X),
                  reads=["psT2"], writes=["gm3"])
            mk.op("vector", lambda v: v.tensor_copy(sconst[0:1, 3:4], self.rowmx[0:1, 1:2]),
                  reads=["gm3"], writes=["gm3b"], lp=True)
            mk.op("tensor", lambda t: t.matmul(self.PS_C[:, 511:512], self.onesb[0:1, :],
                                               sconst[0:1, 3:4], start=True, stop=True),
                  reads=["gm3b", "ones"], writes=["psC"])
            mk.op("scalar", lambda sc: sc.activation(sc1b[:, 2:3], self.PS_C[:, 511:512],
                                                     ACT.Identity, scale=-1.0),
                  reads=["psC"], writes=["c3n"])
            for r in range(32):
                mk.op("scalar", lambda sc, r=r: sc.activation(
                    s3raw[:, r * 256:(r + 1) * 256], s3raw[:, r * 256:(r + 1) * 256], ACT.Exp,
                    bias=sc1b[:, 2:3]),
                    reads=["s3raw", "c3n"], writes=["s3raw"], lp=True)
            for mc in range(2):
                PS = self.PS_C if mc == 0 else self.PS_D
                pk = "psC" if mc == 0 else "psD"
                ob = 0 if hh == 0 else 65
                dcol = (1, 65) if hh == 0 else (0, 64)
                ocol = 0 if hh == 0 else 64
                def mmav(t, mc=mc, ob=ob, PS=PS):
                    for r in range(32):
                        li = t.matmul(PS[:, 0:65],
                                      s3raw[:, r * 256 + mc * 128: r * 256 + mc * 128 + 128],
                                      vtok2[:, r * 130 + ob: r * 130 + ob + 65],
                                      start=(r == 0), stop=(r == 31))
                    return li
                mk.op("tensor", mmav, reads=["s3raw", "vtok2", "vtok2i"], writes=[pk],
                      tag=f"av3 h{h}m{mc}")
                mk.op("vector", lambda v, PS=PS, ocol=ocol: v.reciprocal(
                    self.rowmx[:, 0:1], PS[:, ocol:ocol + 1]),
                    reads=[pk], writes=["rowmx"], lp=True)
                mk.op("vector", lambda v, mc=mc, PS=PS, dcol=dcol: v.tensor_scalar_mul(
                    yav[:, mc * 65: mc * 65 + 64], PS[:, dcol[0]:dcol[1]], self.rowmx[:, 0:1]),
                    reads=[pk, "rowmx"], writes=["yav"], lp=True)
            mk.op("vector", lambda v: v.memset(yv[:], 1.0), writes=["yv"], tag="yv ones")
            for oc2 in range(2):
                def mmy(t, oc2=oc2):
                    for kc2 in range(2):
                        li = t.matmul(self.PS_E[:, 0:64],
                                      zzT[:, kc2 * 256 + oc2 * 128: kc2 * 256 + oc2 * 128 + 128],
                                      yav[:, kc2 * 65: kc2 * 65 + 64],
                                      start=(kc2 == 0), stop=(kc2 == 1))
                    return li
                mk.op("tensor", mmy, reads=["zzT", "yav"], writes=["psE"], tag=f"Y h{h}")
                mk.op("vector", lambda v, oc2=oc2: v.tensor_copy(yv[:, oc2 * 65: oc2 * 65 + 64],
                                                                 self.PS_E[:, 0:64]),
                      reads=["psE", "yv"], writes=["yv"], lp=True)
            mk.op("vector", lambda v: v.memset(sc1b[:, 3:4], NEG), writes=["accm1"])
            for mc in range(2):
                for ns in range(4):
                    def mmsim1(t, mc=mc, ns=ns, d0=d0):
                        t.matmul(self.PS_A[:, 0:512], kl2[d0:d0 + 64, mc * 128:(mc + 1) * 128],
                                 q2[d0:d0 + 64, ns * 1024: ns * 1024 + 512], start=True, stop=True)
                        return t.matmul(self.PS_A[:, 512:1024], kl2[d0:d0 + 64, mc * 128:(mc + 1) * 128],
                                        q2[d0:d0 + 64, ns * 1024 + 512:(ns + 1) * 1024],
                                        start=True, stop=True)
                    mk.op("tensor", mmsim1, reads=["kl2", "q2"], writes=["psA"],
                          tag=f"sim1 h{h}m{mc}n{ns}")
                    mk.op("vector", lambda v, mc=mc, ns=ns: v.tensor_copy(
                        sraw[:, mc * T + ns * 1024: mc * T + (ns + 1) * 1024], self.PS_A[:, 0:1024]),
                        reads=["psA"], writes=["sraw"], lp=True)
                mk.op("vector", lambda v, mc=mc: v.reduce_max(self.rowmx[:, 0:1],
                                                              sraw[:, mc * T:(mc + 1) * T],
                                                              axis=AX.X),
                      reads=["sraw"], writes=["rowmx"])
                mk.op("vector", lambda v: v.tensor_max(sc1b[:, 3:4], sc1b[:, 3:4],
                                                       self.rowmx[:, 0:1]),
                      reads=["rowmx", "accm1"], writes=["accm1"])
            mk.op("vector", lambda v: v.tensor_copy(sconst[:, 1:2], sc1b[:, 3:4]),
                  reads=["accm1"], writes=["accm3b"], lp=True)
            mk.op("tensor", lambda t: t.transpose(self.PS_T[0:1, 128:256], sconst[:, 1:2],
                                                  self.ident[:]),
                  reads=["accm3b"], writes=["psT2"], lp=True)
            mk.op("vector", lambda v: v.reduce_max(self.rowmx[0:1, 1:2], self.PS_T[0:1, 128:256],
                                                   axis=AX.X),
                  reads=["psT2"], writes=["gm3"])
            mk.op("vector", lambda v: v.tensor_copy(sconst[0:1, 3:4], self.rowmx[0:1, 1:2]),
                  reads=["gm3"], writes=["gm3b"], lp=True)
            mk.op("tensor", lambda t: t.matmul(self.PS_C[:, 511:512], self.onesb[0:1, :],
                                               sconst[0:1, 3:4], start=True, stop=True),
                  reads=["gm3b", "ones"], writes=["psC"])
            mk.op("scalar", lambda sc: sc.activation(sc1b[:, 4:5], self.PS_C[:, 511:512],
                                                     ACT.Identity, scale=-1.0),
                  reads=["psC"], writes=["c1n"])
            for mc in range(2):
                mk.op("scalar", lambda sc, mc=mc: sc.activation(
                    sraw[:, mc * T:(mc + 1) * T], sraw[:, mc * T:(mc + 1) * T], ACT.Exp,
                    bias=sc1b[:, 4:5]),
                    reads=["sraw", "c1n"], writes=["sraw"], lp=True)
            for r in range(32):
                def mmo1(t, r=r):
                    for mc in range(2):
                        li = t.matmul(self.PS_E[:, 128:193],
                                      sraw[:, mc * T + r * 128: mc * T + r * 128 + 128],
                                      yv[:, mc * 65:(mc + 1) * 65],
                                      start=(mc == 0), stop=(mc == 1))
                    return li
                mk.op("tensor", mmo1, reads=["sraw", "yv"], writes=["psE"], tag=f"o1 h{h}r{r}")
                mk.op("vector", lambda v: v.reciprocal(self.rowmx[:, 0:1], self.PS_E[:, 192:193]),
                      reads=["psE"], writes=["rowmx"], lp=True)
                mk.op("vector", lambda v, r=r: v.tensor_scalar_mul(
                    outk[:, r * 64:(r + 1) * 64], self.PS_E[:, 128:192], self.rowmx[:, 0:1]),
                    reads=["psE", "rowmx"], writes=["outk"], lp=True)
            for which in range(3):
                mk.op("vector", lambda v, which=which: v.memset(
                    t2cur[:, which * 128:(which + 1) * 128], 0.0), writes=["t2"])
            js = {0: range(13), 1: range(0, 6), 2: range(7, 13)}
            for which in range(3):
                for j in js[which]:
                    mk.op("vector", lambda v, which=which, j=j, h=h: v.scalar_tensor_tensor(
                        t2cur[:, which * 128:(which + 1) * 128],
                        spc[:, (which * 13 + j) * 128:(which * 13 + j) * 128 + 128],
                        wcb[:, h * 13 + j: h * 13 + j + 1],
                        t2cur[:, which * 128:(which + 1) * 128],
                        ALU.mult, ALU.add),
                        reads=["spc", "wcb", "t2"], writes=["t2"], lp=True)
            for r in range(32):
                def mmcv(t, r=r):
                    first = True
                    li = None
                    if r > 0:
                        li = t.matmul(self.PS_E[:, 256:320], t2cur[:, 128:256],
                                      outk[:, (r - 1) * 64: r * 64], start=True, stop=False)
                        first = False
                    li = t.matmul(self.PS_E[:, 256:320], t2cur[:, 0:128],
                                  outk[:, r * 64:(r + 1) * 64], start=first, stop=(r == 31))
                    if r < 31:
                        li = t.matmul(self.PS_E[:, 256:320], t2cur[:, 256:384],
                                      outk[:, (r + 1) * 64:(r + 2) * 64], start=False, stop=True)
                    return li
                mk.op("tensor", mmcv, reads=["t2", "outk"], writes=["psE"], tag=f"cv h{h}r{r}")
                mk.op("vector", lambda v, r=r: v.tensor_add(restk[:, r * 64:(r + 1) * 64],
                                                            self.PS_E[:, 256:320],
                                                            outk[:, r * 64:(r + 1) * 64]),
                      reads=["psE", "outk"], writes=["restk"], tag="s=o+r")
            for r0 in range(0, 32, 4):
                def mmTs(t, r0=r0):
                    for j in range(4):
                        li = t.transpose(self.PS_T[0:64, j * 256:j * 256 + 128],
                                         restk[:, (r0 + j) * 64:(r0 + j + 1) * 64],
                                         self.ident[:])
                    return li
                mk.op("tensor", mmTs, reads=["restk"], writes=["psT"], lp=True)
                def vTs(v, r0=r0, d0=d0):
                    for j in range(4):
                        li = v.tensor_copy(
                            sfm2[d0:d0 + 64, (r0 + j) * 128:(r0 + j + 1) * 128],
                            self.PS_T[0:64, j * 256:j * 256 + 128])
                    return li
                mk.op("vector", vTs, reads=["psT"], writes=["sfm2"])
        mk.op("sync", lambda ss, hp=hp: ss.dma_start(
            out=self.sfmh[hp * 128:(hp + 1) * 128, :], in_=sfm2[:]),
            reads=["sfm2"], writes=["sfmh"], dma=True, tag=f"st sfm hp{hp}")

    def _s5_fc(self, sfmc, wfct, resw7):
        mk = self.mk
        for oc in range(NCH):
            for ns in range(4):
                PS = self.PS_A if ns % 2 == 0 else self.PS_B
                pk = "psA" if ns % 2 == 0 else "psB"
                for ic in range(NCH):
                    mk.op("gpsimd", lambda g, ic=ic, ns=ns: g.dma_start(
                        out=sfmc[:, ns * 1024:(ns + 1) * 1024],
                        in_=self.sfmh[128 * ic:128 * ic + 128, ns * 1024:(ns + 1) * 1024]),
                        writes=["sfmc"], reads=["sfmcc"], dma=True, tag=f"ldsfm o{oc}n{ns}i{ic}")
                    mk.op("gpsimd", lambda g, ic=ic, oc=oc: g.dma_start(
                        out=wfct[:, (ic % 2) * 128:(ic % 2) * 128 + 128],
                        in_=self.wfc[128 * ic:128 * ic + 128, 128 * oc:128 * oc + 128]),
                        writes=[("wfct", ic % 2)], reads=[("wfctc", ic % 2)], dma=True)
                    def mmfc(t, ic=ic, ns=ns, PS=PS):
                        last = (ic == NCH - 1)
                        t.matmul(PS[:, 0:512], wfct[:, (ic % 2) * 128:(ic % 2) * 128 + 128],
                                 sfmc[:, ns * 1024: ns * 1024 + 512],
                                 start=(ic == 0), stop=last)
                        return t.matmul(PS[:, 512:1024], wfct[:, (ic % 2) * 128:(ic % 2) * 128 + 128],
                                        sfmc[:, ns * 1024 + 512:(ns + 1) * 1024],
                                        start=(ic == 0), stop=last)
                    mk.op("tensor", mmfc, reads=[("wfct", ic % 2), "sfmc"],
                          writes=[pk, ("wfctc", ic % 2), "sfmcc"], tag=f"fc o{oc}n{ns}i{ic}")
                mk.op("sync", lambda ss, oc=oc, ns=ns: ss.dma_start(
                    out=resw7[:, 0:1024], in_=self.resbuf[:, oc * T + ns * 1024: oc * T + (ns + 1) * 1024]),
                    reads=[("resb", 2 * ns), ("resb", 2 * ns + 1)], writes=["resw7"], dma=True)
                mk.op("vector", lambda v, oc=oc, PS=PS: v.scalar_tensor_tensor(
                    resw7[:, 0:1024], PS[:, 0:1024], self.bfcv[:, oc:oc + 1], resw7[:, 0:1024],
                    ALU.add, ALU.add),
                    reads=[pk, "resw7", "bfcv"], writes=["resw7"], tag="fc+res")
                mk.op("sync", lambda ss, oc=oc, ns=ns: ss.dma_start(
                    out=self.resbuf[:, oc * T + ns * 1024: oc * T + (ns + 1) * 1024],
                    in_=resw7[:, 0:1024]),
                    reads=["resw7"], writes=[("resb", 2 * ns), ("resb", 2 * ns + 1)], dma=True)

    def s6(self):
        mk = self.mk
        QT = 1024
        sc8 = []
        rwhi = self.sb("rwhi", [128, NCH * 8], BF16, scope=sc8)
        rwlo = self.sb("rwlo", [128, NCH * 8], BF16, scope=sc8)
        rwf = self.sb("rwf", [128, NCH * 8], F32, scope=sc8)
        rsb = self.sb("rsb", [128, 8], BF16, scope=sc8)
        biasb = self.sb("biasb", [128, 8], F32, scope=sc8)
        rrow = self.sb("rrow", [1, 8], BF16, scope=sc8)
        identf = self.sb("identf", [128, 128], F32, scope=sc8)
        mk.op("sync", lambda ss: ss.dma_start(out=identf[:], in_=self.cident),
              writes=["identf"], dma=True)
        for hc in range(NCH):
            mk.op("sync", lambda ss, hc=hc: ss.dma_start(
                out=rwf[:, hc * 8:(hc + 1) * 8], in_=self.rwT[128 * hc:128 * hc + 128, :]),
                writes=["rwf"], dma=True)
        mk.op("vector", lambda v: v.tensor_copy(rwhi[:], rwf[:]),
              reads=["rwf"], writes=["rwhi"], lp=True)
        mk.op("vector", lambda v: v.tensor_sub(rwlo[:], rwf[:], rwhi[:]),
              reads=["rwf", "rwhi"], writes=["rwlo"], lp=True)
        def mmrs(t):
            for hc in range(NCH):
                li = t.matmul(self.PS_C[0:1, 0:8], rwhi[:, hc * 8:(hc + 1) * 8],
                              self.onesb[:, 0:1], start=(hc == 0), stop=(hc == NCH - 1))
            return li
        def mmrs2(t):
            for hc in range(NCH):
                li = t.matmul(self.PS_C[0:1, 0:8], self.onesb[:, 0:1],
                              rwhi[:, hc * 8:(hc + 1) * 8], start=(hc == 0), stop=(hc == NCH - 1))
            return li
        mk.op("tensor", mmrs2, reads=["rwhi", "ones"], writes=["psC"], tag="rs")
        mk.op("vector", lambda v: v.tensor_copy(rrow[0:1, :], self.PS_C[0:1, 0:8]),
              reads=["psC"], writes=["rrow"], lp=True)
        mk.op("tensor", lambda t: t.matmul(self.PS_C[:, 8:16], self.onesb[0:1, :], rrow[0:1, :],
                                           start=True, stop=True),
              reads=["rrow", "ones"], writes=["psC"])
        mk.op("vector", lambda v: v.tensor_copy(rsb[:], self.PS_C[:, 8:16]),
              reads=["psC"], writes=["rsb"], lp=True)
        onesf128 = self.sb("onesf128", [128, 1], F32, scope=sc8)
        rbrow = self.sb("rbrow", [1, 8], F32, scope=sc8)
        mk.op("vector", lambda v: v.memset(onesf128[:], 1.0), writes=["onesf128"])
        mk.op("sync", lambda ss: ss.dma_start(out=rbrow[:], in_=self.rbias),
              writes=["rbrow"], dma=True)
        onesfr = self.sb("onesfr", [1, 128], F32, scope=sc8)
        mk.op("vector", lambda v: v.memset(onesfr[:], 1.0), writes=["onesfr"])
        mk.op("tensor", lambda t: t.matmul(self.PS_C[:, 24:32], onesfr[0:1, :], rbrow[0:1, :],
                                           start=True, stop=True),
              reads=["rbrow", "onesfr"], writes=["psC"], tag="biasb2")
        mk.op("vector", lambda v: v.tensor_copy(biasb[:], self.PS_C[:, 24:32]),
              reads=["psC"], writes=["biasb"])
        for q in range(4):
            self._s6_quarter(q, sc8, rwhi, rwlo, rsb, biasb, identf)
        self.close_scope(sc8)

    def _s6_quarter(self, q, sc8, rwhi, rwlo, rsb, biasb, identf):
        mk = self.mk
        QT = 1024
        t0 = q * QT
        scq = []
        resq = self.sb(f"resq{q}", [128, NCH * QT], F32, scope=scq)
        xn3q = self.sb(f"xn3q{q}", [128, NCH * QT], BF16, scope=scq)
        for c in range(NCH):
            mk.op("sync", lambda ss, c=c: ss.dma_start(
                out=resq[:, c * QT:(c + 1) * QT],
                in_=self.resbuf[:, c * T + t0: c * T + t0 + QT]),
                reads=[("resb", 2 * q), ("resb", 2 * q + 1)], writes=["resq"], dma=True,
                tag=f"ldres8 q{q}")
        for half in range(2):
            o0 = half * 512
            for c in range(NCH):
                mk.op("vector", lambda v, c=c, o0=o0: v.tensor_copy(
                    xn3q[:, c * QT + o0: c * QT + o0 + 512],
                    resq[:, c * QT + o0: c * QT + o0 + 512]),
                    reads=["resq"], writes=["xn3q"], lp=True)
                mk.op("vector", lambda v, c=c, o0=o0: v.tensor_mul(
                    self.sqt[:, 0:512], xn3q[:, c * QT + o0: c * QT + o0 + 512],
                    xn3q[:, c * QT + o0: c * QT + o0 + 512]),
                    reads=["xn3q"], writes=["sqt"], lp=True)
                def mm_stats(t, c=c, o0=o0):
                    last = (c == NCH - 1)
                    t.matmul(self.PS_A[:, 0:512], self.onesb[:],
                             xn3q[:, c * QT + o0: c * QT + o0 + 512], start=(c == 0), stop=last)
                    return t.matmul(self.PS_B[:, 0:512], self.onesb[:], self.sqt[:, 0:512],
                                    start=(c == 0), stop=last)
                mk.op("tensor", mm_stats, reads=["xn3q", "sqt", "ones"], writes=["psA", "psB"])
            mk.op("scalar", lambda sc: sc.mul(self.mu_t[:, 0:512], self.PS_A[:, 0:512], 1.0 / D),
                  reads=["psA"], writes=["mu"], lp=True)
            mk.op("vector", lambda v: v.tensor_mul(self.var_t[:, 0:512], self.mu_t[:, 0:512],
                                                   self.mu_t[:, 0:512]),
                  reads=["mu"], writes=["var"])
            mk.op("vector", lambda v: v.scalar_tensor_tensor(self.var_t[:, 0:512],
                                                             self.PS_B[:, 0:512], 1.0 / D,
                                                             self.var_t[:, 0:512],
                                                             ALU.mult, ALU.subtract),
                  reads=["psB", "var"], writes=["var"])
            mk.op("scalar", lambda sc: sc.activation(self.var_t[:, 0:512], self.var_t[:, 0:512],
                                                     ACT.Sqrt, bias=self.epsv[:, 0:1]),
                  reads=["var", "epsv"], writes=["var"])
            mk.op("vector", lambda v: v.reciprocal(self.inv_t[:, 0:512], self.var_t[:, 0:512]),
                  reads=["var"], writes=["inv"], lp=True)
            for c in range(NCH):
                mk.op("vector", lambda v, c=c, o0=o0: v.tensor_sub(
                    xn3q[:, c * QT + o0: c * QT + o0 + 512],
                    xn3q[:, c * QT + o0: c * QT + o0 + 512], self.mu_t[:, 0:512]),
                    reads=["mu", "xn3q"], writes=["xn3q"])
                mk.op("vector", lambda v, c=c, o0=o0: v.tensor_mul(
                    xn3q[:, c * QT + o0: c * QT + o0 + 512],
                    xn3q[:, c * QT + o0: c * QT + o0 + 512], self.inv_t[:, 0:512]),
                    reads=["inv", "xn3q"], writes=["xn3q"])
        cwrow = self.sb(f"cwrow{q}", [8, QT], BF16, scope=scq)
        scr = []
        yhi = self.sb(f"yhi{q}", [128, NCH * QT], BF16, scope=scr)
        ylo = self.sb(f"ylo{q}", [128, NCH * QT], BF16, scope=scr)
        lgt = self.sb(f"lgt{q}", [128, 16], F32, scope=scr)
        cwt = self.sb(f"cwt{q}", [128, 8], BF16, scope=scr)
        sqc = self.sb(f"sqc{q}", [128, 128], BF16, scope=scr)
        ztq = self.sb(f"ztq{q}", [128, 16], F32, scope=scr)
        for c in range(NCH):
            mk.op("vector", lambda v, c=c: v.tensor_copy(yhi[:, c * QT:(c + 1) * QT],
                                                         resq[:, c * QT:(c + 1) * QT]),
                  reads=["resq"], writes=["yhi"], lp=True)
            mk.op("vector", lambda v, c=c: v.tensor_sub(ylo[:, c * QT:(c + 1) * QT],
                                                        resq[:, c * QT:(c + 1) * QT],
                                                        yhi[:, c * QT:(c + 1) * QT]),
                  reads=["resq", "yhi"], writes=["ylo"], lp=True)
        for tt in range(QT // 128):
            def mmlg(t, tt=tt):
                for c in range(NCH):
                    sl = slice(c * QT + tt * 128, c * QT + tt * 128 + 128)
                    first = (c == 0)
                    last = (c == NCH - 1)
                    t.matmul(self.PS_E[:, 0:8], yhi[:, sl], rwhi[:, c * 8:(c + 1) * 8],
                             start=first, stop=False)
                    t.matmul(self.PS_E[:, 0:8], ylo[:, sl], rwhi[:, c * 8:(c + 1) * 8],
                             start=False, stop=False)
                    t.matmul(self.PS_E[:, 0:8], yhi[:, sl], rwlo[:, c * 8:(c + 1) * 8],
                             start=False, stop=last)
                    t.matmul(self.PS_D[:, 4:5], yhi[:, sl], self.onesb[:, 0:1],
                             start=first, stop=False)
                    li = t.matmul(self.PS_D[:, 4:5], ylo[:, sl], self.onesb[:, 0:1],
                                  start=False, stop=last)
                return li
            mk.op("tensor", mmlg, reads=["yhi", "ylo", "rwhi", "rwlo", "ones"],
                  writes=["psE", "psD"], tag=f"lg q{q}t{tt}")
            for c in range(NCH):
                mk.op("vector", lambda v, c=c, tt=tt: v.tensor_mul(
                    sqc[:], resq[:, c * QT + tt * 128: c * QT + tt * 128 + 128],
                    resq[:, c * QT + tt * 128: c * QT + tt * 128 + 128]),
                    reads=["resq"], writes=["sqc"], lp=True)
                mk.op("tensor", lambda t, c=c: t.matmul(
                    self.PS_D[:, 0:1], sqc[:], self.onesb[:, 0:1],
                    start=(c == 0), stop=(c == NCH - 1)),
                    reads=["sqc", "ones"], writes=["psD"], tag="sumsq")
            mk.op("scalar", lambda sc: sc.mul(self.rowmx[:, 0:1], self.PS_D[:, 4:5], 1.0 / D),
                  reads=["psD"], writes=["rowmx"])
            mk.op("vector", lambda v: v.tensor_mul(self.rowmx[:, 1:2], self.rowmx[:, 0:1],
                                                   self.rowmx[:, 0:1]),
                  reads=["rowmx"], writes=["rowmx2"])
            mk.op("vector", lambda v: v.scalar_tensor_tensor(self.rowmx[:, 1:2], self.PS_D[:, 0:1],
                                                             1.0 / D, self.rowmx[:, 1:2],
                                                             ALU.mult, ALU.subtract),
                  reads=["psD", "rowmx2"], writes=["rowmx2"])
            mk.op("scalar", lambda sc: sc.activation(self.rowmx[:, 1:2], self.rowmx[:, 1:2],
                                                     ACT.Sqrt, bias=self.epsv[:, 0:1]),
                  reads=["rowmx2", "epsv"], writes=["rowmx2"])
            mk.op("vector", lambda v: v.reciprocal(self.rowmx[:, 2:3], self.rowmx[:, 1:2]),
                  reads=["rowmx2"], writes=["rowmx3"])
            mk.op("scalar", lambda sc: sc.mul(self.rowmx[:, 3:4], self.rowmx[:, 0:1], -1.0),
                  reads=["rowmx"], writes=["rowmx4"])
            mk.op("vector", lambda v: v.scalar_tensor_tensor(lgt[:, 0:8], rsb[:],
                                                             self.rowmx[:, 3:4], self.PS_E[:, 0:8],
                                                             ALU.mult, ALU.add),
                  reads=["rsb", "rowmx4", "psE"], writes=["lgt"])
            mk.op("vector", lambda v: v.tensor_scalar_mul(lgt[:, 0:8], lgt[:, 0:8],
                                                          self.rowmx[:, 2:3]),
                  reads=["lgt", "rowmx3"], writes=["lgt"])
            mk.op("scalar", lambda sc: sc.activation(lgt[:, 8:16], lgt[:, 0:8], ACT.Sigmoid),
                  reads=["lgt"], writes=["lgt"])
            mk.op("vector", lambda v: v.tensor_add(lgt[:, 0:8], lgt[:, 8:16], biasb[:]),
                  reads=["lgt", "biasb"], writes=["lgt"])
            mk.op("vector", lambda v: v.tensor_add(
                ztq[:, 0:4], lgt[:, 0:8].rearrange("p (g two) -> p g two", two=2)[:, :, 0],
                lgt[:, 0:8].rearrange("p (g two) -> p g two", two=2)[:, :, 1]),
                reads=["lgt"], writes=["zt6"])
            mk.op("vector", lambda v: v.reduce_max(self.rowmx[:, 0:1], ztq[:, 0:4], axis=AX.X),
                  reads=["zt6"], writes=["rowmx"])
            mk.op("vector", lambda v: v.tensor_scalar(ztq[:, 4:8], ztq[:, 0:4],
                                                      self.rowmx[:, 0:1], None, ALU.is_ge),
                  reads=["zt6", "rowmx"], writes=["zt6b"])
            mk.op("vector", lambda v: v.scalar_tensor_tensor(ztq[:, 8:12],
                                                             ztq[:, 4:8], -30000.0,
                                                             ztq[:, 0:4],
                                                             ALU.mult, ALU.add),
                  reads=["zt6", "zt6b"], writes=["zt6c"])
            mk.op("vector", lambda v: v.reduce_max(self.rowmx[:, 1:2], ztq[:, 8:12], axis=AX.X),
                  reads=["zt6c"], writes=["rowmx2"])
            mk.op("vector", lambda v: v.tensor_scalar(ztq[:, 4:8], ztq[:, 0:4],
                                                      self.rowmx[:, 1:2], None, ALU.is_ge),
                  reads=["zt6", "rowmx2"], writes=["zt6b"])
            mk.op("vector", lambda v: v.tensor_copy(
                ztq[:, 8:16].rearrange("p (g two) -> p g two", two=2)[:, :, 0],
                ztq[:, 4:8]),
                reads=["zt6b"], writes=["zt6d"])
            mk.op("vector", lambda v: v.tensor_copy(
                ztq[:, 8:16].rearrange("p (g two) -> p g two", two=2)[:, :, 1],
                ztq[:, 4:8]),
                reads=["zt6b"], writes=["zt6d"])
            mk.op("vector", lambda v: v.tensor_mul(lgt[:, 0:8], lgt[:, 0:8], ztq[:, 8:16]),
                  reads=["lgt", "zt6d"], writes=["lgt"])
            mk.op("vector", lambda v: v.reduce_max(self.rowmx[:, 0:1], lgt[:, 0:8], axis=AX.X),
                  reads=["lgt"], writes=["rowmx"])
            mk.op("vector", lambda v: v.tensor_scalar(ztq[:, 0:8], lgt[:, 0:8],
                                                      self.rowmx[:, 0:1], None, ALU.is_ge),
                  reads=["lgt", "rowmx"], writes=["zt6"])
            mk.op("vector", lambda v: v.tensor_mul(ztq[:, 8:16], lgt[:, 8:16],
                                                   ztq[:, 0:8]),
                  reads=["lgt", "zt6"], writes=["zt6d"])
            mk.op("vector", lambda v: v.reduce_sum(self.rowmx[:, 1:2], ztq[:, 8:16], axis=AX.X),
                  reads=["zt6d"], writes=["rowmx2"])
            mk.op("vector", lambda v: v.scalar_tensor_tensor(lgt[:, 0:8], ztq[:, 0:8],
                                                             -30000.0, lgt[:, 0:8],
                                                             ALU.mult, ALU.add),
                  reads=["lgt", "zt6"], writes=["lgt"])
            mk.op("vector", lambda v: v.reduce_max(self.rowmx[:, 0:1], lgt[:, 0:8], axis=AX.X),
                  reads=["lgt"], writes=["rowmx"])
            mk.op("vector", lambda v: v.tensor_scalar(ztq[:, 8:16], lgt[:, 0:8],
                                                      self.rowmx[:, 0:1], None, ALU.is_ge),
                  reads=["lgt", "rowmx"], writes=["zt6e"])
            mk.op("vector", lambda v: v.tensor_mul(lgt[:, 0:8],
                                                   lgt[:, 8:16], ztq[:, 8:16]),
                  reads=["lgt", "zt6e"], writes=["lgt2"])
            mk.op("vector", lambda v: v.reduce_sum(self.rowmx[:, 2:3], lgt[:, 0:8], axis=AX.X),
                  reads=["lgt2"], writes=["rowmx3"])
            mk.op("vector", lambda v: v.tensor_add(self.rowmx[:, 3:4], self.rowmx[:, 1:2],
                                                   self.rowmx[:, 2:3]),
                  reads=["rowmx2", "rowmx3"], writes=["rowmx4"])
            mk.op("vector", lambda v: v.tensor_single_scalar(self.rowmx[:, 3:4], self.rowmx[:, 3:4],
                                                             1e-20, ALU.add),
                  reads=["rowmx4"], writes=["rowmx4"])
            mk.op("vector", lambda v: v.reciprocal(self.rowmx[:, 3:4], self.rowmx[:, 3:4]),
                  reads=["rowmx4"], writes=["rowmx4"])
            mk.op("vector", lambda v: v.tensor_scalar_mul(cwt[:], ztq[:, 0:8],
                                                          self.rowmx[:, 1:2]),
                  reads=["zt6", "rowmx2"], writes=["cwt"], lp=True)
            mk.op("vector", lambda v: v.scalar_tensor_tensor(cwt[:], ztq[:, 8:16],
                                                             self.rowmx[:, 2:3], cwt[:],
                                                             ALU.mult, ALU.add),
                  reads=["zt6e", "rowmx3", "cwt"], writes=["cwt"], lp=True)
            mk.op("vector", lambda v: v.tensor_scalar_mul(cwt[:], cwt[:], self.rowmx[:, 3:4]),
                  reads=["cwt", "rowmx4"], writes=["cwt"], lp=True)
            mk.op("tensor", lambda t: t.transpose(self.PS_T[0:8, 0:128], cwt[:], self.ident[:]),
                  reads=["cwt", "ident"], writes=["psT"], lp=True)
            mk.op("vector", lambda v, tt=tt: v.tensor_copy(cwrow[0:8, tt * 128:(tt + 1) * 128],
                                                           self.PS_T[0:8, 0:128]),
                  reads=["psT"], writes=["cwrow"])
        self.close_scope(scr)
        cwrh = self.dram(f"cwrh{q}", [8, QT], BF16)
        mk.op("sync", lambda ss: ss.dma_start(out=cwrh, in_=cwrow[0:8, :]),
              reads=["cwrow"], writes=["cwrh"], dma=True)
        scg = []
        gbuf = self.sb(f"gbuf{q}", [128, 4 * QT], BF16, scope=scg)
        ubuf = self.sb(f"ubuf{q}", [128, 4 * QT], BF16, scope=scg)
        wband = self.sb(f"wband{q}", [128, 2 * D], BF16, scope=scg)
        cwb = self.sb(f"cwb{q}", [128, QT], BF16, scope=scg)
        acc = self.sb(f"acc{q}", [128, NCH * QT], F32, scope=scg)
        wdb = self.sb(f"wdb{q}", [128, 4 * QT], BF16, scope=scg)
        cwr0 = self.sb(f"cwr0{q}", [1, QT], BF16, scope=scg)
        for e in range(8):
            mk.op("sync", lambda ss, e=e: ss.dma_start(out=cwr0[:], in_=cwrh[e:e + 1, :]),
                  reads=["cwrh", "cwr0c"], writes=["cwr0"], dma=True)
            mk.op("tensor", lambda t, e=e: t.matmul(self.PS_C[:, 0:512], self.onesb[0:1, :],
                                                    cwr0[0:1, 0:512], start=True, stop=True),
                  reads=["cwr0", "ones"], writes=["psC"])
            mk.op("tensor", lambda t, e=e: t.matmul(self.PS_D[:, 0:512], self.onesb[0:1, :],
                                                    cwr0[0:1, 512:1024], start=True, stop=True),
                  reads=["cwr0", "ones"], writes=["psD", "cwr0c"])
            mk.op("vector", lambda v: v.tensor_copy(cwb[:, 0:512], self.PS_C[:, 0:512]),
                  reads=["psC"], writes=["cwb"], lp=True)
            mk.op("vector", lambda v: v.tensor_copy(cwb[:, 512:1024], self.PS_D[:, 0:512]),
                  reads=["psD"], writes=["cwb"], lp=True)
            for which, W, dbuf, dkey in ((0, self.wgT, gbuf, "gbuf"), (1, self.wuT, ubuf, "ubuf")):
                for ic in range(4):
                    par = (which * 4 + ic) % 2
                    PSg = self.PS_A if par == 0 else self.PS_B
                    pkg = "psA" if par == 0 else "psB"
                    half = par
                    mk.op("gpsimd", lambda g, W=W, e=e, ic=ic, half=half: g.dma_start(
                        out=wband[:, half * D:half * D + D], in_=W[e, ic]),
                        writes=[("wband", half)], reads=[("wbandc", half)], dma=True)
                    def mmgu(t, PSg=PSg, half=half):
                        for hc in range(NCH):
                            last = (hc == NCH - 1)
                            sl = half * D + hc * 128
                            t.matmul(PSg[:, 0:512], wband[:, sl:sl + 128],
                                     xn3q[:, hc * QT: hc * QT + 512], start=(hc == 0), stop=last)
                            li = t.matmul(PSg[:, 512:1024], wband[:, sl:sl + 128],
                                          xn3q[:, hc * QT + 512:(hc + 1) * QT],
                                          start=(hc == 0), stop=last)
                        return li
                    mk.op("tensor", mmgu, reads=[("wband", half), "xn3q"],
                          writes=[pkg, ("wbandc", half)],
                          tag=f"gu q{q}e{e}w{which}i{ic}")
                    if which == 0:
                        mk.op("scalar", lambda sc, ic=ic, PSg=PSg: sc.activation(
                            gbuf[:, ic * QT:(ic + 1) * QT], PSg[:, 0:1024], ACT.Silu),
                            reads=[pkg], writes=["gbuf"], lp=True)
                    else:
                        mk.op("vector", lambda v, ic=ic, PSg=PSg: v.tensor_copy(
                            ubuf[:, ic * QT:(ic + 1) * QT], PSg[:, 0:1024]),
                            reads=[pkg], writes=["ubuf"], lp=True)
            for ic in range(4):
                mk.op("vector", lambda v, ic=ic: v.tensor_mul(
                    gbuf[:, ic * QT:(ic + 1) * QT], gbuf[:, ic * QT:(ic + 1) * QT],
                    ubuf[:, ic * QT:(ic + 1) * QT]),
                    reads=["gbuf", "ubuf"], writes=["gbuf"], lp=True)
                mk.op("vector", lambda v, ic=ic: v.tensor_mul(
                    gbuf[:, ic * QT:(ic + 1) * QT],
                    gbuf[:, ic * QT:(ic + 1) * QT], cwb[:]),
                    reads=["gbuf", "cwb"], writes=["gbuf"], lp=True)
            for ic in range(4):
                mk.op("gpsimd", lambda g, e=e, ic=ic: g.dma_start(
                    out=wdb[:, ic * QT:(ic + 1) * QT],
                    in_=self.wdT[e, 128 * ic:128 * ic + 128, :]),
                    writes=[("wdb", ic)], reads=[("wdbc", ic)], dma=True, tag=f"ldwd e{e}i{ic}")
            for hc in range(NCH):
                PSe = self.PS_A if hc % 2 == 0 else self.PS_B
                pke = "psA" if hc % 2 == 0 else "psB"
                def mmeo(t, hc=hc, PSe=PSe):
                    for ic in range(4):
                        last = (ic == 3)
                        t.matmul(PSe[:, 0:512], wdb[:, ic * QT + hc * 128: ic * QT + hc * 128 + 128],
                                 gbuf[:, ic * QT: ic * QT + 512], start=(ic == 0), stop=last)
                        li = t.matmul(PSe[:, 512:1024],
                                      wdb[:, ic * QT + hc * 128: ic * QT + hc * 128 + 128],
                                      gbuf[:, ic * QT + 512:(ic + 1) * QT],
                                      start=(ic == 0), stop=last)
                    return li
                mk.op("tensor", mmeo, reads=["gbuf"] + [("wdb", ic) for ic in range(4)],
                      writes=[pke], tag=f"eo q{q}e{e}h{hc}")
                if e == 0:
                    mk.op("vector", lambda v, hc=hc, PSe=PSe: v.tensor_copy(
                        acc[:, hc * QT:(hc + 1) * QT], PSe[:, 0:1024]),
                        reads=[pke], writes=["acc"])
                else:
                    mk.op("vector", lambda v, hc=hc, PSe=PSe: v.tensor_add(
                        acc[:, hc * QT:(hc + 1) * QT], acc[:, hc * QT:(hc + 1) * QT],
                        PSe[:, 0:1024]),
                        reads=[pke, "acc"], writes=["acc"])
            mk.op("vector", lambda v: v.engine_nop(), reads=["acc"],
                  writes=[("wdbc", ic2) for ic2 in range(4)])
        for sip in range(2):
            for which, W, dbuf in ((0, self.sgT, gbuf), (1, self.suT, ubuf)):
                for ic4 in range(4):
                    sic = sip * 4 + ic4
                    par = (which * 4 + ic4) % 2
                    PSg = self.PS_A if par == 0 else self.PS_B
                    pkg = "psA" if par == 0 else "psB"
                    half = par
                    mk.op("gpsimd", lambda g, W=W, sic=sic, half=half: g.dma_start(
                        out=wband[:, half * D:half * D + D], in_=W[sic]),
                        writes=[("wband", half)], reads=[("wbandc", half)], dma=True)
                    def mmgs(t, PSg=PSg, half=half):
                        for hc in range(NCH):
                            last = (hc == NCH - 1)
                            sl = half * D + hc * 128
                            t.matmul(PSg[:, 0:512], wband[:, sl:sl + 128],
                                     xn3q[:, hc * QT: hc * QT + 512], start=(hc == 0), stop=last)
                            li = t.matmul(PSg[:, 512:1024], wband[:, sl:sl + 128],
                                          xn3q[:, hc * QT + 512:(hc + 1) * QT],
                                          start=(hc == 0), stop=last)
                        return li
                    mk.op("tensor", mmgs, reads=[("wband", half), "xn3q"],
                          writes=[pkg, ("wbandc", half)], tag=f"sh q{q}p{sip}w{which}i{ic4}")
                    if which == 0:
                        mk.op("scalar", lambda sc, ic4=ic4, PSg=PSg: sc.activation(
                            gbuf[:, ic4 * QT:(ic4 + 1) * QT], PSg[:, 0:1024], ACT.Silu),
                            reads=[pkg], writes=["gbuf"], lp=True)
                    else:
                        mk.op("vector", lambda v, ic4=ic4, PSg=PSg: v.tensor_copy(
                            ubuf[:, ic4 * QT:(ic4 + 1) * QT], PSg[:, 0:1024]),
                            reads=[pkg], writes=["ubuf"], lp=True)
            for ic4 in range(4):
                mk.op("vector", lambda v, ic4=ic4: v.tensor_mul(
                    gbuf[:, ic4 * QT:(ic4 + 1) * QT], gbuf[:, ic4 * QT:(ic4 + 1) * QT],
                    ubuf[:, ic4 * QT:(ic4 + 1) * QT]),
                    reads=["gbuf", "ubuf"], writes=["gbuf"], lp=True)
            for ic4 in range(4):
                sic = sip * 4 + ic4
                mk.op("gpsimd", lambda g, sic=sic, ic4=ic4: g.dma_start(
                    out=wdb[:, ic4 * QT:(ic4 + 1) * QT],
                    in_=self.sdT[128 * sic:128 * sic + 128, :]),
                    writes=[("wdb", ic4)], reads=[("wdbc", ic4)], dma=True)
            for hc in range(NCH):
                PSe = self.PS_A if hc % 2 == 0 else self.PS_B
                pke = "psA" if hc % 2 == 0 else "psB"
                def mmsd(t, hc=hc, PSe=PSe):
                    for ic4 in range(4):
                        last = (ic4 == 3)
                        t.matmul(PSe[:, 0:512],
                                 wdb[:, ic4 * QT + hc * 128: ic4 * QT + hc * 128 + 128],
                                 gbuf[:, ic4 * QT: ic4 * QT + 512], start=(ic4 == 0), stop=last)
                        li = t.matmul(PSe[:, 512:1024],
                                      wdb[:, ic4 * QT + hc * 128: ic4 * QT + hc * 128 + 128],
                                      gbuf[:, ic4 * QT + 512:(ic4 + 1) * QT],
                                      start=(ic4 == 0), stop=last)
                    return li
                mk.op("tensor", mmsd, reads=["gbuf"] + [("wdb", ic2) for ic2 in range(4)],
                      writes=[pke], tag=f"sd q{q}p{sip}h{hc}")
                mk.op("vector", lambda v, hc=hc, PSe=PSe: v.tensor_add(
                    acc[:, hc * QT:(hc + 1) * QT], acc[:, hc * QT:(hc + 1) * QT],
                    PSe[:, 0:1024]),
                    reads=[pke, "acc"], writes=["acc"])
            mk.op("vector", lambda v: v.engine_nop(), reads=["acc"],
                  writes=[("wdbc", ic2) for ic2 in range(4)] + [("wbandc", 0), ("wbandc", 1)])
        outt = self.sb(f"outt{q}", [128, D], F32, scope=scg)
        for c in range(NCH):
            mk.op("vector", lambda v, c=c: v.tensor_add(acc[:, c * QT:(c + 1) * QT],
                                                        acc[:, c * QT:(c + 1) * QT],
                                                        resq[:, c * QT:(c + 1) * QT]),
                  reads=["acc", "resq"], writes=["acc"], tag="y4")
        for tt in range(QT // 128):
            for c in range(NCH):
                mk.op("tensor", lambda t, tt=tt, c=c: t.transpose(
                    self.PS_C[:, 0:128], acc[:, c * QT + tt * 128: c * QT + tt * 128 + 128],
                    identf[:]),
                    reads=["acc", "identf"], writes=["psC"])
                mk.op("vector", lambda v, c=c: v.tensor_copy(outt[:, c * 128:(c + 1) * 128],
                                                             self.PS_C[:, 0:128]),
                      reads=["psC"], writes=["outt"])
            mk.op("sync", lambda ss, tt=tt: ss.dma_start(
                out=self.OUT[t0 + tt * 128: t0 + tt * 128 + 128, :], in_=outt[:]),
                reads=["outt"], writes=["OUT"], dma=True, tag=f"stOUT q{q}t{tt}")
        self.close_scope(scg)
        self.close_scope(scq)

    def finalize(self):
        self.mk.lower(verbose=True)

def build(dbg=(), stages=("s1",), own_b=0, s2_batches=None):
    kb = KB(dbg=dbg)
    kb.declare()
    kb.s0()
    if "s1" in stages:
        kb.s1()
    kb.close_scope(kb.scopeA)
    if "s2" in stages:
        kb.s2(own_b, batches=s2_batches)
    if "s4" in stages:
        kb.s3_s4()
    if "s5" in stages:
        kb.s5()
    if "s6" in stages:
        kb.s6()
        if "l1a" in kb.dbg_names:
            mk = kb.mk
            d6 = kb.out("dbg_y3", [128, NCH * T])
            for c in range(NCH):
                mk.op("sync", lambda ss, c=c: ss.dma_start(out=d6[:, c * T:(c + 1) * T],
                                                           in_=kb.resbuf[:, c * T:(c + 1) * T]),
                      reads=[("resb", sp) for sp in range(NSPAN)], dma=True)
        if "moe" in kb.dbg_names:
            mk = kb.mk
            d5 = kb.out("dbg_y2", [128, NCH * T])
            for c in range(NCH):
                mk.op("sync", lambda ss, c=c: ss.dma_start(out=d5[:, c * T:(c + 1) * T],
                                                           in_=kb.resbuf[:, c * T:(c + 1) * T]),
                      reads=[("resb", sp) for sp in range(NSPAN)], dma=True)
        if "route" in kb.dbg_names:
            mk = kb.mk
            d1 = kb.out("dbg_cnt", [16, NB]);  d2 = kb.out("dbg_prow", [T], I32)
            d3 = kb.out("dbg_vrow", [T], F32); d4 = kb.out("dbg_ohe", [16, T])
            mk.op("sync", lambda ss: ss.dma_start(out=d1, in_=kb.cnt_sb[:]), reads=["cnt"], dma=True)
            mk.op("sync", lambda ss: ss.dma_start(out=d2[None, :], in_=kb.pbufh[None, :]), reads=["pbufh"], dma=True)
            mk.op("sync", lambda ss: ss.dma_start(out=d3[None, :], in_=kb.vbufh[None, :]), reads=["vbufh"], dma=True)
            mk.op("sync", lambda ss: ss.dma_start(out=d4, in_=kb.ohe[:]), reads=["ohe"], dma=True)
    kb.finalize()
    return kb


import concourse.mybir as _mb
from concourse.bass_utils import run_bass_kernel_spmd

_KB_CACHE = {}

def _conv_pieces():
    scur = np.zeros((13, 128, 128), np.float32)
    sprev = np.zeros((13, 128, 128), np.float32)
    snext = np.zeros((13, 128, 128), np.float32)
    for j in range(13):
        o = j - 6
        for cc in range(128):
            r = cc + o
            if 0 <= r < 128: scur[j, r, cc] = 1.0
            r = cc + o + 128
            if 0 <= r < 128: sprev[j, r, cc] = 1.0
            r = cc + o - 128
            if 0 <= r < 128: snext[j, r, cc] = 1.0
    return scur, sprev, snext


def kernel(**inputs):
    inp = inputs
    x = np.ascontiguousarray(np.asarray(inp["L0_attn_sequence_output"], dtype=np.float32))
    kb = build(dbg=(), stages=("s1", "s2", "s4", "s5", "s6"), own_b=0, s2_batches=None)
    _mb.codegen_inst_isa_subclasses(kb.nc)
    c = host_consts()
    scur, sprev, snext = _conv_pieces()
    CLc = CL
    common = {
        "Wq": inp["L0_attn_Wq"], "Wk": inp["L0_attn_Wk"], "Wv": inp["L0_attn_Wv"],
        "WqT": np.ascontiguousarray(np.asarray(inp["L0_attn_Wq"]).T),
        "WkT": np.ascontiguousarray(np.asarray(inp["L0_attn_Wk"]).T),
        "WvT": np.ascontiguousarray(np.asarray(inp["L0_attn_Wv"]).T),
        "Wg": inp["L0_ffn_Wg"],
        "bq_pc": np.ascontiguousarray(np.asarray(inp["L0_attn_bq"]).reshape(8, 128).T),
        "bk_pc": np.ascontiguousarray(np.asarray(inp["L0_attn_bk"]).reshape(8, 128).T),
        "bv_pc": np.ascontiguousarray(np.asarray(inp["L0_attn_bv"]).reshape(8, 128).T),
        "c_band": c["band"], "c_ident": c["ident"], "c_onesmat": c["onesmat"],
        "ebase": (np.arange(16, dtype=np.float32) * CLc).reshape(16, 1),
        "w1": inp["L0_ffn_w1"],
        "w2": np.ascontiguousarray(
            np.asarray(inp["L0_ffn_w2"]).reshape(16, 16, 2, 128, 2, 512)
            .transpose(0, 4, 1, 3, 2, 5).reshape(16, 2, 16, 128, 1024)),
        "w_qkv": inp["L1_attn_w_qkv"],
        "w_fc": inp["L1_attn_w_fc"],
        "bfc_pc": np.ascontiguousarray(np.asarray(inp["L1_attn_b_fc"]).reshape(8, 128).T),
        "wconv_row": np.ascontiguousarray(np.asarray(inp["L1_attn_w_conv"]).reshape(1, 208)),
        "c_scur": scur, "c_sprev": sprev, "c_snext": snext,
        "rwT": np.ascontiguousarray(np.asarray(inp["L1_ffn_router_w"]).T),
        "rbias": np.ascontiguousarray(np.asarray(inp["L1_ffn_bias"]).reshape(1, 8)),
        "wgT": np.ascontiguousarray(np.asarray(inp["L1_ffn_wg"]).transpose(0, 2, 1)
                                    .reshape(8, 8, 128, 4, 128).transpose(0, 3, 2, 1, 4)
                                    .reshape(8, 4, 128, 1024)),
        "wuT": np.ascontiguousarray(np.asarray(inp["L1_ffn_wu"]).transpose(0, 2, 1)
                                    .reshape(8, 8, 128, 4, 128).transpose(0, 3, 2, 1, 4)
                                    .reshape(8, 4, 128, 1024)),
        "wdT": np.ascontiguousarray(np.asarray(inp["L1_ffn_wd"]).transpose(0, 2, 1)),
        "sgT": np.ascontiguousarray(np.asarray(inp["L1_ffn_sg"]).T
                                    .reshape(8, 128, 8, 128).transpose(2, 1, 0, 3)
                                    .reshape(8, 128, 1024)),
        "suT": np.ascontiguousarray(np.asarray(inp["L1_ffn_su"]).T
                                    .reshape(8, 128, 8, 128).transpose(2, 1, 0, 3)
                                    .reshape(8, 128, 1024)),
        "sdT": np.ascontiguousarray(np.asarray(inp["L1_ffn_sd"]).T),
    }
    common = {k: np.ascontiguousarray(np.asarray(v, dtype=np.float32)) for k, v in common.items()}
    x_fm = np.ascontiguousarray(x.transpose(0, 2, 1))
    in_maps = []
    for b in range(NB):
        order = [b] + [bb for bb in range(NB) if bb != b]
        xall = np.ascontiguousarray(x_fm[order])
        pref = np.zeros((16, NB), np.float32)
        for pos, gb in enumerate(order):
            if gb < b:
                pref[:, pos] = 1.0
        m = dict(common)
        m["xall_fm"] = xall
        m["xown_fm"] = x_fm[b]
        m["prefmask"] = pref
        in_maps.append(m)
    res = run_bass_kernel_spmd(kb.nc, in_maps, core_ids=list(range(NB)))
    out = np.stack([np.asarray(res.results[b]["out"]) for b in range(NB)], axis=0)
    return out.astype(np.float32)
